# version 04
# baseline (speedup 1.0000x reference)
import numpy as np
import ml_dtypes
import concourse.bass as bass
import concourse.mybir as mybir
from concourse.bass_utils import run_bass_kernel_spmd

F32 = mybir.dt.float32
BF16 = mybir.dt.bfloat16
I32 = mybir.dt.int32
U8 = mybir.dt.uint8
AF = mybir.ActivationFunctionType
ALU = mybir.AluOpType
AX = mybir.AxisListType
NPBF = ml_dtypes.bfloat16

P = 128
S = 2048
D = 1024
H = 3072
NT = S // P
ND = D // P
NH = H // P
NG = NT // 4
SEQ_PER_CORE = 2
N_CORES = 8
INV_SQRT_DK = 1.0 / 32.0
DEQ = 1.0 / 127.0
EPS = 1e-5

BLOCK_METHOD = {"pe": "tensor", "act": "scalar", "dve": "vector", "pool": "gpsimd", "sp": "sync"}

DEBUG = False


class _Op:
    __slots__ = ("i", "eng", "fn", "deps", "name", "flag", "dma_key", "sem_id", "val", "know")

    def __init__(self, i, eng, fn, deps, name, dma_key):
        self.i, self.eng, self.fn, self.deps, self.name, self.dma_key = i, eng, fn, deps, name, dma_key
        self.flag = False
        self.sem_id = None
        self.val = None
        self.know = None


class Builder:

    def __init__(self, nc):
        self.nc = nc
        self.ops = []
        self.lw = {}
        self.rd = {}
        self.name_keys = {}
        self.alias_deps = {}

    @staticmethod
    def _key_name(k):
        return k[0] if isinstance(k, tuple) else k

    def op(self, eng, fn, reads=(), writes=(), name="", dma_key=None):
        i = len(self.ops)
        deps = set()
        for r in reads:
            if r in self.lw:
                deps.add(self.lw[r])
        for w in writes:
            if w in self.lw:
                deps.add(self.lw[w])
            for r_ in self.rd.get(w, ()):
                deps.add(r_)
            ad = self.alias_deps.get(self._key_name(w))
            if ad:
                deps |= ad
        deps.discard(i)
        o = _Op(i, eng, fn, deps, name, dma_key)
        self.ops.append(o)
        for r in reads:
            self.rd.setdefault(r, []).append(i)
            self.name_keys.setdefault(self._key_name(r), set()).add(r)
        for w in writes:
            self.lw[w] = i
            self.rd[w] = []
            self.name_keys.setdefault(self._key_name(w), set()).add(w)
        return i

    def dma(self, queue, fn, reads=(), writes=(), key=None, name=""):
        if key is None:
            key = "dq_" + queue
        return self.op(queue, fn, reads, writes, name, dma_key=key)

    _fence_fns = {}

    def set_fence_fns(self, fns):
        self._fence_fns = fns

    def fence(self, eng, keys):
        return self.op(eng, self._fence_fns[eng], reads=list(keys), writes=list(keys), name="fence")

    def pending_ops_of(self, tensor_name):
        out = set()
        for k in self.name_keys.get(tensor_name, ()):
            if k in self.lw:
                out.add(self.lw[k])
            out.update(self.rd.get(k, ()))
        return out

    def add_alias_deps(self, tensor_name, dep_ops):
        if dep_ops:
            self.alias_deps.setdefault(tensor_name, set()).update(dep_ops)

    def finalize_and_emit(self, final_wait_keys=()):
        nc = self.nc
        ops = self.ops
        for o in ops:
            for d in o.deps:
                p = ops[d]
                if p.dma_key is None and not (p.eng == o.eng and p.eng in ("pe", "sp")):
                    p.flag = True
        eng_sems = {}
        dma_sems = {}
        for o in ops:
            if o.dma_key is not None:
                if o.dma_key not in dma_sems:
                    dma_sems[o.dma_key] = nc.alloc_semaphore(name=("sd_" + str(o.dma_key))[:30])
            elif o.flag and o.eng not in eng_sems:
                eng_sems[o.eng] = nc.alloc_semaphore(name="se_" + o.eng)
        eng_counts = {}
        dma_counts = {}
        for o in ops:
            if o.dma_key is not None:
                c = dma_counts.get(o.dma_key, 0) + 16
                dma_counts[o.dma_key] = c
                o.sem_id = ("d", o.dma_key)
                o.val = c
            elif o.flag:
                c = eng_counts.get(o.eng, 0) + 1
                eng_counts[o.eng] = c
                o.sem_id = ("e", o.eng)
                o.val = c
        last_on_eng = {}
        for o in ops:
            know = dict(last_on_eng.get(o.eng, {}))
            for d in sorted(o.deps):
                p = ops[d]
                if p.sem_id is None:
                    continue
                if know.get(p.sem_id, 0) < p.val:
                    know[p.sem_id] = p.val
                for k, v in p.know.items():
                    if know.get(k, 0) < v:
                        know[k] = v
            o.know = know
            last_on_eng[o.eng] = know

        ops_by_eng = {}
        for o in ops:
            ops_by_eng.setdefault(o.eng, []).append(o)
        n_waits = [0]

        def emit_for(eng_key):
            def body(eng):
                know = {}
                for o in ops_by_eng[eng_key]:
                    for d in sorted(o.deps):
                        p = ops[d]
                        if p.sem_id is None:
                            continue
                        if know.get(p.sem_id, 0) >= p.val:
                            continue
                        sem = dma_sems[p.sem_id[1]] if p.sem_id[0] == "d" else eng_sems[p.sem_id[1]]
                        eng.wait_ge(sem, p.val)
                        n_waits[0] += 1
                        know[p.sem_id] = p.val
                        for k, v in p.know.items():
                            if know.get(k, 0) < v:
                                know[k] = v
                    ins = o.fn(eng)
                    if o.sem_id is not None:
                        ins.then_inc(dma_sems[o.sem_id[1]] if o.sem_id[0] == "d" else eng_sems[o.sem_id[1]],
                                     16 if o.sem_id[0] == "d" else 1)
                    for k, v in o.know.items():
                        if know.get(k, 0) < v:
                            know[k] = v
                if eng_key == "sp":
                    for k in final_wait_keys:
                        if k in dma_counts and know.get(("d", k), 0) < dma_counts[k]:
                            eng.wait_ge(dma_sems[k], dma_counts[k])
            return body

        with nc.Block() as block:
            keys = set(ops_by_eng) | ({"sp"} if final_wait_keys else set())
            for eng_key in keys:
                ops_by_eng.setdefault(eng_key, [])
                getattr(block, BLOCK_METHOD[eng_key])(emit_for(eng_key))
        return {"n_ops": len(ops), "n_waits": n_waits[0],
                "n_by_eng": {k: len(v) for k, v in ops_by_eng.items()}}


class Arena:

    def __init__(self, nc, b):
        self.nc = nc
        self.b = b
        avail = nc.sbuf_bytes_remaining - 2048
        self.backing = nc.alloc_sbuf_tensor("arena", [P, avail], U8)
        self.base = nc.lookup_mloc(self.backing).addr
        self.size = avail
        self.free_list = [[0, avail, set()]]
        self.live = {}
        self.peak = 0

    def alloc(self, name, shape, dtype):
        nbytes = int(np.prod(shape[1:])) * mybir.dt.size(dtype)
        nbytes = (nbytes + 31) // 32 * 32
        for ent in sorted(self.free_list, key=lambda e: e[1]):
            if ent[1] >= nbytes:
                off = ent[0]
                deps = set(ent[2])
                ent[0] += nbytes
                ent[1] -= nbytes
                if ent[1] == 0:
                    self.free_list.remove(ent)
                for nm, (o2, n2) in self.live.items():
                    assert off + nbytes <= o2 or o2 + n2 <= off, \
                        f"arena overlap: {name}@({off},{nbytes}) vs {nm}@({o2},{n2})"
                h = self.nc.alloc_sbuf_tensor_at(name, list(shape), dtype, offset=self.base + off)
                self.live[name] = (off, nbytes)
                self.peak = max(self.peak, max((o + n for o, n in self.live.values())))
                self.b.add_alias_deps(name, deps)
                return h
        raise MemoryError(f"arena: no space for {name} ({nbytes}B/part); "
                          f"free={[(e[0], e[1]) for e in self.free_list]}")

    def free(self, name):
        off, nbytes = self.live.pop(name)
        deps = self.b.pending_ops_of(name)
        self.free_list.append([off, nbytes, deps])
        self.free_list.sort(key=lambda e: e[0])
        merged = [self.free_list[0]]
        for ent in self.free_list[1:]:
            last = merged[-1]
            if last[0] + last[1] == ent[0]:
                last[1] += ent[1]
                last[2] = last[2] | ent[2]
            else:
                merged.append(ent)
        self.free_list = merged


def build_program():
    nc = bass.Bass("TRN2", target_bir_lowering=False)

    x_d = nc.dram_tensor("x", [SEQ_PER_CORE, S, D], F32, kind="ExternalInput")
    wq_d = nc.dram_tensor("wq", [D, D], F32, kind="ExternalInput")
    wk_d = nc.dram_tensor("wk", [D, D], F32, kind="ExternalInput")
    wv_d = nc.dram_tensor("wv", [D, D], F32, kind="ExternalInput")
    wo_d = nc.dram_tensor("wo", [D, D], F32, kind="ExternalInput")
    wg_d = nc.dram_tensor("wg", [H, D], I32, kind="ExternalInput")
    wu_d = nc.dram_tensor("wu", [H, D], I32, kind="ExternalInput")
    wd_d = nc.dram_tensor("wd", [D, H], I32, kind="ExternalInput")
    out_d = nc.dram_tensor("out", [SEQ_PER_CORE, S, D], F32, kind="ExternalOutput")
    y_dram = nc.dram_tensor("ydram", [S, D], F32, kind="ExternalOutput" if DEBUG else "Internal")
    dbg = {}
    if DEBUG:
        dbg["en"] = nc.dram_tensor("dbg_en", [NT, P, S], BF16, kind="ExternalOutput")
        dbg["rsum"] = nc.dram_tensor("dbg_rsum", [NT, P, 1], F32, kind="ExternalOutput")
        dbg["ao"] = nc.dram_tensor("dbg_ao", [ND, P, S], BF16, kind="ExternalOutput")
        dbg["r1"] = nc.dram_tensor("dbg_r1", [NT, P, D], F32, kind="ExternalOutput")
        dbg["scores"] = nc.dram_tensor("dbg_scores", [NT, P, S], F32, kind="ExternalOutput")
        dbg["ln"] = nc.dram_tensor("dbg_ln", [NT, P, 8], F32, kind="ExternalOutput")
        dbg["sq"] = nc.dram_tensor("dbg_sq", [NT, P, D], F32, kind="ExternalOutput")

    PF = nc.alloc_psum_tensor("PF", [P, 3072], F32)
    PB = nc.alloc_psum_tensor("PB", [P, 2048], BF16)
    assert nc.lookup_mloc(PF).bank == 0 and nc.lookup_mloc(PB).bank == 6

    def pf(col0, width):
        ap = PF[:, col0:col0 + width]
        keys = tuple(f"F{bk}" for bk in range(col0 // 512, (col0 + width - 1) // 512 + 1))
        return ap, keys

    def pbank(half, col0, width):
        ap = PB[:, half * 1024 + col0: half * 1024 + col0 + width]
        return ap, (f"B{6 + half}",)

    b = Builder(nc)
    ar = Arena(nc, b)

    identf = ar.alloc("identf", [P, P], F32)
    identb = ar.alloc("identb", [P, P], BF16)
    maskt = ar.alloc("maskt", [P, 4, 512], F32)
    nmax = ar.alloc("nmax", [P, 2], F32)
    nmaxs = ar.alloc("nmaxs", [P, 2], F32)
    rsum = ar.alloc("rsum", [P, 2], F32)
    rinv = ar.alloc("rinv", [P, 2], F32)
    s1 = ar.alloc("s1", [P, 2], F32)
    s2 = ar.alloc("s2", [P, 2], F32)
    negmean = ar.alloc("negmean", [P, 2], F32)
    nmean2 = ar.alloc("nmean2", [P, 2], F32)
    var2 = ar.alloc("var2", [P, 2], F32)
    lnl = ar.alloc("lnl", [P, 2], F32)
    rstd = ar.alloc("rstd", [P, 2], F32)
    sqbuf = ar.alloc("sqbuf", [P, D], F32)
    epsb = ar.alloc("epsb", [P, 1], F32)
    xs = ar.alloc("xs", [P, 2, D], F32)
    b.op("pool", lambda e: e.memset(epsb[:], EPS), writes=[("epsb",)])
    fb = ar.alloc("fb", [P, 6], F32)
    b.set_fence_fns({
        "dve": lambda e: e.memset(fb[:, 1:2], 0.0),
        "act": lambda e: e.memzero(fb[:, 3:4]),
        "pool": lambda e: e.memset(fb[:, 5:6], 0.0),
    })

    b.op("pool", lambda e: e.memset(identf[:], 0.0), writes=[("identf",)])
    b.op("pool", lambda e: e.affine_select(out=identf[:], in_=identf[:], compare_op=ALU.not_equal,
                                           fill=1.0, base=0, pattern=[[-1, P]], channel_multiplier=1),
         reads=[("identf",)], writes=[("identf",)])
    b.op("pool", lambda e: e.tensor_copy(identb[:], identf[:]), reads=[("identf",)], writes=[("identb",)])
    for j in range(4):
        b.op("pool", lambda e, j=j: e.memset(maskt[:, j, :], 0.0), writes=[("maskt", j)])
        b.op("pool", lambda e, j=j: e.affine_select(out=maskt[:, j, :], in_=maskt[:, j, :],
                                                    compare_op=ALU.is_ge, fill=-1e30, base=j * P,
                                                    pattern=[[-1, 512]], channel_multiplier=1),
             reads=[("maskt", j)], writes=[("maskt", j)])

    def emit_ln(src_ap, src_keys, dst_ap, dst_keys, sl, stats_eng, tap_t=None):
        def tap(col, key):
            if tap_t is None:
                return
            src = {"s1": s1, "s2": s2, "negmean": negmean, "nmean2": nmean2,
                   "var2": var2, "lnl": lnl, "rstd": rstd}[key]
            b.dma("sp", lambda e: e.dma_start(out=dbg["ln"].ap()[tap_t, :, col:col + 1], in_=src[:, sl:sl + 1],
                                              allow_slow_non_contiguous=True),
                  reads=[(key, sl)], key="dbg")

        b.op(stats_eng, lambda e: e.reduce_sum(s1[:, sl:sl + 1], src_ap, axis=AX.X),
             reads=src_keys, writes=[("s1", sl)])
        b.op(stats_eng, lambda e: e.scalar_tensor_tensor(sqbuf[:], src_ap, 1.0, src_ap, ALU.mult, ALU.mult,
                                                         accum_out=s2[:, sl:sl + 1]),
             reads=src_keys, writes=[("sqbuf",), ("s2", sl)])
        tap(0, "s1")
        tap(1, "s2")
        if tap_t is not None:
            b.dma("sp", lambda e: e.dma_start(out=dbg["sq"].ap()[tap_t], in_=sqbuf[:]),
                  reads=[("sqbuf",)], key="dbg")
        b.op("dve", lambda e: e.tensor_scalar_mul(negmean[:, sl:sl + 1], s1[:, sl:sl + 1], -1.0 / D),
             reads=[("s1", sl)], writes=[("negmean", sl)])
        b.op("dve", lambda e: e.tensor_scalar(nmean2[:, sl:sl + 1], negmean[:, sl:sl + 1],
                                              negmean[:, sl:sl + 1], -1.0, ALU.mult, ALU.mult),
             reads=[("negmean", sl)], writes=[("nmean2", sl)])
        b.op("dve", lambda e: e.tensor_scalar(var2[:, sl:sl + 1], s2[:, sl:sl + 1],
                                              1.0 / D, nmean2[:, sl:sl + 1], ALU.mult, ALU.add),
             reads=[("s2", sl), ("nmean2", sl)], writes=[("var2", sl)])
        b.op("act", lambda e: e.activation(lnl[:, sl:sl + 1], var2[:, sl:sl + 1], AF.Ln, bias=epsb[:], scale=1.0),
             reads=[("var2", sl), ("epsb",)], writes=[("lnl", sl)])
        b.op("act", lambda e: e.activation(rstd[:, sl:sl + 1], lnl[:, sl:sl + 1], AF.Exp, scale=-0.5),
             reads=[("lnl", sl)], writes=[("rstd", sl)])
        tap(2, "negmean")
        tap(3, "nmean2")
        tap(4, "var2")
        tap(5, "lnl")
        tap(6, "rstd")
        b.op("dve", lambda e: e.tensor_scalar(dst_ap, src_ap, negmean[:, sl:sl + 1], rstd[:, sl:sl + 1],
                                              ALU.add, ALU.mult),
             reads=list(src_keys) + [("negmean", sl), ("rstd", sl)], writes=dst_keys)

    def emit_seq(s):
        sfx = f"_s{s}"

        xT = ar.alloc("xT" + sfx, [P, ND, S], BF16)
        wqb = ar.alloc("wqb" + sfx, [P, ND, D], BF16)
        wkb = ar.alloc("wkb" + sfx, [P, ND, D], BF16)

        b.dma("pool", lambda e: e.dma_start(out=wqb[:], in_=wq_d.ap().rearrange("(ec p) k -> p ec k", p=P)),
              writes=[("wqb" + sfx,)], key="wq")
        b.dma("pool", lambda e: e.dma_start(out=wkb[:], in_=wk_d.ap().rearrange("(ec p) k -> p ec k", p=P)),
              writes=[("wkb" + sfx,)], key="wk")

        for t in range(NT):
            sl = t % 2
            b.dma("sp", lambda e, t=t, sl=sl: e.dma_start(out=xs[:, sl, :], in_=x_d.ap()[s, t * P:(t + 1) * P, :]),
                  writes=[("xs", sl)], key=f"xs{sl}")
            bk0 = 2 if t % 2 == 0 else 4
            for c in range(ND):
                ap, kk = pf((bk0 + c // 4) * 512 + (c % 4) * P, P)
                b.op("pe", lambda e, ap=ap, sl=sl, c=c: e.transpose(ap, xs[:, sl, c * P:(c + 1) * P], identf[:]),
                     reads=[("xs", sl), ("identf",)], writes=list(kk))
            for half in range(2):
                ap, kk = pf((bk0 + half) * 512, 512)
                dst = xT[:, half * 4:(half + 1) * 4, t * P:(t + 1) * P]
                b.op("act", lambda e, dst=dst, ap=ap: e.copy(dst, ap),
                     reads=list(kk), writes=[("xT" + sfx, half * 4 + c, t) for c in range(4)])

        def project_fmaj(wb, wname, dstname, dst):
            for m in range(ND):
                for half in range(2):
                    for ec in range(ND):
                        for n in (2 * half, 2 * half + 1):
                            ap, kk = pf((n % 4) * 512, 512)
                            b.op("pe", lambda e, ap=ap, m=m, ec=ec, n=n, wb=wb:
                                 e.matmul(ap, wb[:, ec, m * P:(m + 1) * P], xT[:, ec, n * 512:(n + 1) * 512],
                                          start=(ec == 0), stop=(ec == ND - 1)),
                                 reads=[(wname,)] + [("xT" + sfx, ec, 4 * n + i) for i in range(4)],
                                 writes=list(kk))
                    for n in (2 * half, 2 * half + 1):
                        ap, kk = pf((n % 4) * 512, 512)
                        b.op("act", lambda e, dst=dst, m=m, n=n, ap=ap: e.copy(dst[:, m, n * 512:(n + 1) * 512], ap),
                             reads=list(kk), writes=[(dstname, m, n)])

        qT = ar.alloc("qT" + sfx, [P, ND, S], BF16)
        kT = ar.alloc("kT" + sfx, [P, ND, S], BF16)
        project_fmaj(wqb, "wqb" + sfx, "qT" + sfx, qT)
        project_fmaj(wkb, "wkb" + sfx, "kT" + sfx, kT)
        ar.free("wqb" + sfx)
        ar.free("wkb" + sfx)

        wvb = ar.alloc("wvb" + sfx, [P, ND, D], BF16)
        b.dma("pool", lambda e: e.dma_start(out=wvb[:], in_=wv_d.ap().rearrange("(ec p) k -> p ec k", p=P)),
              writes=[("wvb" + sfx,)], key="wv")
        vtm = ar.alloc("v" + sfx, [P, NT, D], BF16)
        for t in range(NT):
            for ec in range(ND):
                for half in range(2):
                    ap, kk = pf(((t % 2) * 2 + half) * 512, 512)
                    b.op("pe", lambda e, ap=ap, ec=ec, t=t, half=half:
                         e.matmul(ap, xT[:, ec, t * P:(t + 1) * P], wvb[:, ec, half * 512:(half + 1) * 512],
                                  start=(ec == 0), stop=(ec == ND - 1)),
                         reads=[("wvb" + sfx,), ("xT" + sfx, ec, t)], writes=list(kk))
            for half in range(2):
                ap, kk = pf(((t % 2) * 2 + half) * 512, 512)
                b.op("act", lambda e, t=t, half=half, ap=ap: e.copy(vtm[:, t, half * 512:(half + 1) * 512], ap),
                     reads=list(kk), writes=[("v" + sfx, t)])
        ar.free("wvb" + sfx)
        ar.free("xT" + sfx)

        aoT = ar.alloc("aoT" + sfx, [P, ND, S], BF16)
        wob = ar.alloc("wob" + sfx, [P, ND, D], BF16)
        b.dma("pool", lambda e: e.dma_start(out=wob[:], in_=wo_d.ap().rearrange("(ec p) k -> p ec k", p=P)),
              writes=[("wob" + sfx,)], key="wo")
        Ebuf = ar.alloc("E" + sfx, [P, S], F32)
        Enb = ar.alloc("En" + sfx, [P, S], BF16)
        ET = ar.alloc("ET" + sfx, [P, NT, 512], BF16)

        for g in range(NG):
            wlen = (g + 1) * 512
            ncs = 4 * (g + 1)
            for j in range(4):
                t = 4 * g + j
                sl = j % 2
                for ec in range(ND):
                    for nb in range(g + 1):
                        ap, kk = pf(nb * 512, 512)
                        b.op("pe", lambda e, ap=ap, ec=ec, t=t, nb=nb:
                             e.matmul(ap, qT[:, ec, t * P:(t + 1) * P], kT[:, ec, nb * 512:(nb + 1) * 512],
                                      start=(ec == 0), stop=(ec == ND - 1)),
                             reads=[("qT" + sfx, ec, t // 4), ("kT" + sfx, ec, nb)], writes=list(kk))
                apm, kkm = pf(g * 512, 512)
                b.op("dve", lambda e, apm=apm, j=j: e.tensor_tensor(apm, apm, maskt[:, j, :], ALU.add),
                     reads=list(kkm) + [("maskt", j)], writes=list(kkm))
                aps, kks = pf(0, wlen)
                b.op("dve", lambda e, aps=aps, sl=sl: e.reduce_max(nmax[:, sl:sl + 1], aps, axis=AX.X, negate=True),
                     reads=list(kks), writes=[("nmax", sl)])
                b.op("dve", lambda e, sl=sl: e.tensor_scalar_mul(nmaxs[:, sl:sl + 1], nmax[:, sl:sl + 1], INV_SQRT_DK),
                     reads=[("nmax", sl)], writes=[("nmaxs", sl)])
                b.op("act", lambda e, aps=aps, sl=sl, wlen=wlen:
                     e.activation(Ebuf[:, 0:wlen], aps, AF.Exp, bias=nmaxs[:, sl:sl + 1], scale=INV_SQRT_DK,
                                  accum_out=rsum[:, sl:sl + 1]),
                     reads=list(kks) + [("nmaxs", sl)], writes=[("E" + sfx,), ("rsum", sl)])
                b.op("dve", lambda e, sl=sl: e.reciprocal(rinv[:, sl:sl + 1], rsum[:, sl:sl + 1]),
                     reads=[("rsum", sl)], writes=[("rinv", sl)])
                b.op("pool", lambda e, sl=sl, wlen=wlen:
                     e.tensor_scalar_mul(Enb[:, 0:wlen], Ebuf[:, 0:wlen], rinv[:, sl:sl + 1]),
                     reads=[("E" + sfx,), ("rinv", sl)], writes=[("En" + sfx,)])
                if DEBUG and s == 1:
                    b.dma("sp", lambda e, t=t, wlen=wlen: e.dma_start(out=dbg["scores"].ap()[t, :, 0:wlen], in_=Ebuf[:, 0:wlen]),
                          reads=[("E" + sfx,)], key="dbg")
                    b.dma("sp", lambda e, t=t, sl=sl: e.dma_start(out=dbg["rsum"].ap()[t], in_=rsum[:, sl:sl + 1]),
                          reads=[("rsum", sl)], key="dbg")
                    b.dma("sp", lambda e, t=t, wlen=wlen: e.dma_start(out=dbg["en"].ap()[t, :, 0:wlen], in_=Enb[:, 0:wlen]),
                          reads=[("En" + sfx,)], key="dbg")
                for c0 in range(0, ncs, 8):
                    cb = min(8, ncs - c0)
                    half = (c0 // 8) % 2
                    for ci in range(cb):
                        cs = c0 + ci
                        apb, kkb = pbank(half, ci * P, P)
                        b.op("pe", lambda e, apb=apb, cs=cs: e.transpose(apb, Enb[:, cs * P:(cs + 1) * P], identb[:]),
                             reads=[("En" + sfx,), ("identb",)], writes=list(kkb))
                    apb, kkb = pbank(half, 0, cb * P)
                    dst = ET[:, c0:c0 + cb, j * P:(j + 1) * P]
                    b.op("dve", lambda e, dst=dst, apb=apb: e.tensor_copy(dst, apb),
                         reads=list(kkb), writes=[("ET" + sfx, c0 + ci, j) for ci in range(cb)])
            for vc in range(ND):
                ap, kk = pf((4 + vc % 2) * 512, 512)
                for cs in range(ncs):
                    b.op("pe", lambda e, ap=ap, cs=cs, vc=vc, ncs=ncs:
                         e.matmul(ap, vtm[:, cs, vc * P:(vc + 1) * P], ET[:, cs, :],
                                  start=(cs == 0), stop=(cs == ncs - 1)),
                         reads=[("v" + sfx, cs)] + [("ET" + sfx, cs, j) for j in range(4)],
                         writes=list(kk))
                b.op("dve", lambda e, ap=ap, vc=vc, g=g: e.tensor_copy(aoT[:, vc, g * 512:(g + 1) * 512], ap),
                     reads=list(kk), writes=[("aoT" + sfx, vc, g)])

        if DEBUG and s == 1:
            for vc in range(ND):
                b.dma("sp", lambda e, vc=vc: e.dma_start(out=dbg["ao"].ap()[vc], in_=aoT[:, vc, :]),
                      reads=[("aoT" + sfx, vc, g) for g in range(NG)], key="dbg")

        ar.free("qT" + sfx)
        ar.free("kT" + sfx)
        ar.free("v" + sfx)
        ar.free("E" + sfx)
        ar.free("En" + sfx)
        ar.free("ET" + sfx)

        r1 = ar.alloc("r1" + sfx, [P, 2, D], F32)
        ybuf = ar.alloc("y" + sfx, [P, 2, D], F32)
        yT = ar.alloc("yT" + sfx, [P, ND, S], BF16)

        for t in range(NT):
            sl = t % 2
            bz = 0 if t % 2 == 0 else 2
            for half in range(2):
                for vc in range(ND):
                    ap, kk = pf((bz + half) * 512, 512)
                    b.op("pe", lambda e, ap=ap, vc=vc, t=t, half=half:
                         e.matmul(ap, aoT[:, vc, t * P:(t + 1) * P], wob[:, vc, half * 512:(half + 1) * 512],
                                  start=(vc == 0), stop=(vc == ND - 1)),
                         reads=[("wob" + sfx,), ("aoT" + sfx, vc, t // 4)], writes=list(kk))
            b.dma("sp", lambda e, t=t, sl=sl: e.dma_start(out=xs[:, sl, :], in_=x_d.ap()[s, t * P:(t + 1) * P, :]),
                  writes=[("xs", sl)], key=f"xs{sl}")
            apz, kkz = pf(bz * 512, 1024)
            b.op("dve", lambda e, apz=apz, sl=sl: e.tensor_tensor(r1[:, sl, :], apz, xs[:, sl, :], ALU.add),
                 reads=list(kkz) + [("xs", sl)], writes=[("r1" + sfx, sl)])
            if DEBUG and s == 1:
                b.dma("sp", lambda e, t=t, sl=sl: e.dma_start(out=dbg["r1"].ap()[t], in_=r1[:, sl, :]),
                      reads=[("r1" + sfx, sl)], key="dbg")
            emit_ln(r1[:, sl, :], [("r1" + sfx, sl)], ybuf[:, sl, :], [("y" + sfx, sl)], sl, "dve",
                    tap_t=(t if (DEBUG and s == 1) else None))
            b.dma("sp", lambda e, t=t, sl=sl: e.dma_start(out=y_dram.ap()[t * P:(t + 1) * P, :], in_=ybuf[:, sl, :]),
                  reads=[("y" + sfx, sl)], writes=[("ydram", t)], key=f"yst{sl}")
            bk0 = 2 if t % 2 == 0 else 4
            for c in range(ND):
                ap, kk = pf((bk0 + c // 4) * 512 + (c % 4) * P, P)
                b.op("pe", lambda e, ap=ap, sl=sl, c=c: e.transpose(ap, ybuf[:, sl, c * P:(c + 1) * P], identf[:]),
                     reads=[("y" + sfx, sl), ("identf",)], writes=list(kk))
            for half in range(2):
                ap, kk = pf((bk0 + half) * 512, 512)
                dst = yT[:, half * 4:(half + 1) * 4, t * P:(t + 1) * P]
                b.op("act", lambda e, dst=dst, ap=ap: e.copy(dst, ap),
                     reads=list(kk), writes=[("yT" + sfx, half * 4 + c, t) for c in range(4)])

        ar.free("aoT" + sfx)
        ar.free("wob" + sfx)
        ar.free("r1" + sfx)
        ar.free("y" + sfx)

        h_actT = ar.alloc("h" + sfx, [P, NH, S], BF16)
        wgb = ar.alloc("wgb" + sfx, [P, 2, D], BF16)
        wub = ar.alloc("wub" + sfx, [P, 2, D], BF16)
        wgT = ar.alloc("wgT" + sfx, [P, 2, D], BF16)
        wuT = ar.alloc("wuT" + sfx, [P, 2, D], BF16)
        sg = ar.alloc("sg" + sfx, [P, 2, 512], BF16)

        for hc in range(NH):
            sl = hc % 2
            b.dma("pool", lambda e, hc=hc, sl=sl: e.dma_start(out=wgb[:, sl, :], in_=wg_d.ap()[hc * P:(hc + 1) * P, :]),
                  writes=[("wgb" + sfx, sl)], key=f"wg{sl}")
            b.dma("pool", lambda e, hc=hc, sl=sl: e.dma_start(out=wub[:, sl, :], in_=wu_d.ap()[hc * P:(hc + 1) * P, :]),
                  writes=[("wub" + sfx, sl)], key=f"wu{sl}")
            for (wsrc, wsname, wt, wtname, half) in ((wgb, "wgb", wgT, "wgT", 0), (wub, "wub", wuT, "wuT", 1)):
                for dc in range(ND):
                    apb, kkb = pbank(half, dc * P, P)
                    b.op("pe", lambda e, apb=apb, wsrc=wsrc, wsname=wsname, sl=sl, dc=dc:
                         e.transpose(apb, wsrc[:, sl, dc * P:(dc + 1) * P], identb[:]),
                         reads=[(wsname + sfx, sl), ("identb",)], writes=list(kkb))
                apb, kkb = pbank(half, 0, D)
                b.op("dve", lambda e, wt=wt, sl=sl, apb=apb: e.tensor_copy(wt[:, sl, :], apb),
                     reads=list(kkb), writes=[(wtname + sfx, sl)])
            for n in range(4):
                for dc in range(ND):
                    apg, kkg = pf((n % 2) * 512, 512)
                    b.op("pe", lambda e, apg=apg, dc=dc, n=n, sl=sl:
                         e.matmul(apg, wgT[:, sl, dc * P:(dc + 1) * P], yT[:, dc, n * 512:(n + 1) * 512],
                                  start=(dc == 0), stop=(dc == ND - 1)),
                         reads=[("wgT" + sfx, sl)] + [("yT" + sfx, dc, 4 * n + i) for i in range(4)],
                         writes=list(kkg))
                for dc in range(ND):
                    apu, kku = pf((2 + n % 2) * 512, 512)
                    b.op("pe", lambda e, apu=apu, dc=dc, n=n, sl=sl:
                         e.matmul(apu, wuT[:, sl, dc * P:(dc + 1) * P], yT[:, dc, n * 512:(n + 1) * 512],
                                  start=(dc == 0), stop=(dc == ND - 1)),
                         reads=[("wuT" + sfx, sl)] + [("yT" + sfx, dc, 4 * n + i) for i in range(4)],
                         writes=list(kku))
                apg, kkg = pf((n % 2) * 512, 512)
                apu, kku = pf((2 + n % 2) * 512, 512)
                b.op("act", lambda e, apg=apg, n=n: e.activation(sg[:, n % 2, :], apg, AF.Silu, scale=DEQ),
                     reads=list(kkg), writes=[("sg" + sfx, n % 2)])
                b.op("dve", lambda e, apu=apu, n=n, hc=hc:
                     e.tensor_tensor(h_actT[:, hc, n * 512:(n + 1) * 512], apu, sg[:, n % 2, :], ALU.mult),
                     reads=list(kku) + [("sg" + sfx, n % 2)], writes=[("h" + sfx, hc, n)])

        ar.free("yT" + sfx)
        ar.free("wgb" + sfx)
        ar.free("wub" + sfx)
        ar.free("wgT" + sfx)
        ar.free("wuT" + sfx)
        ar.free("sg" + sfx)

        wdb = ar.alloc("wdb" + sfx, [P, 2, ND, P], BF16)
        wdT = ar.alloc("wdT" + sfx, [P, NH, D], BF16)
        ys = ar.alloc("ys" + sfx, [P, 2, D], F32)
        r2 = ar.alloc("r2" + sfx, [P, 2, D], F32)
        outb = ar.alloc("outb" + sfx, [P, 2, D], F32)

        for hc in range(NH):
            sl = hc % 2
            src = wd_d.ap()[:, hc * P:(hc + 1) * P].rearrange("(dc p) h -> p dc h", p=P)
            b.dma("pool", lambda e, src=src, sl=sl: e.dma_start(out=wdb[:, sl], in_=src),
                  writes=[("wdb" + sfx, sl)], key=f"wd{sl}")
            for dc in range(ND):
                apb, kkb = pbank(sl, dc * P, P)
                b.op("pe", lambda e, apb=apb, sl=sl, dc=dc: e.transpose(apb, wdb[:, sl, dc, :], identb[:]),
                     reads=[("wdb" + sfx, sl), ("identb",)], writes=list(kkb))
            apb, kkb = pbank(sl, 0, D)
            b.op("dve", lambda e, apb=apb, hc=hc: e.tensor_copy(wdT[:, hc, :], apb),
                 reads=list(kkb), writes=[("wdT" + sfx, hc)])

        for t in range(NT):
            sl = t % 2
            bz = 0 if t % 2 == 0 else 2
            b.dma("sp", lambda e, t=t, sl=sl: e.dma_start(out=ys[:, sl, :], in_=y_dram.ap()[t * P:(t + 1) * P, :]),
                  reads=[("ydram", t)], writes=[("ys" + sfx, sl)], key=f"ys{sl}")
            for hc in range(NH):
                for half in range(2):
                    ap, kk = pf((bz + half) * 512, 512)
                    b.op("pe", lambda e, ap=ap, hc=hc, t=t, half=half:
                         e.matmul(ap, h_actT[:, hc, t * P:(t + 1) * P], wdT[:, hc, half * 512:(half + 1) * 512],
                                  start=(hc == 0), stop=(hc == NH - 1)),
                         reads=[("h" + sfx, hc, t // 4), ("wdT" + sfx, hc)], writes=list(kk))
            apz, kkz = pf(bz * 512, 1024)
            b.op("dve", lambda e, apz=apz, sl=sl:
                 e.scalar_tensor_tensor(r2[:, sl, :], apz, DEQ * DEQ, ys[:, sl, :], ALU.mult, ALU.add),
                 reads=list(kkz) + [("ys" + sfx, sl)], writes=[("r2" + sfx, sl)])
            emit_ln(r2[:, sl, :], [("r2" + sfx, sl)], outb[:, sl, :], [("outb" + sfx, sl)], sl, "dve")
            b.dma("sp", lambda e, t=t, sl=sl: e.dma_start(out=out_d.ap()[s, t * P:(t + 1) * P, :], in_=outb[:, sl, :]),
                  reads=[("outb" + sfx, sl)], key=f"ost{sl}")

        ar.free("wdb" + sfx)
        ar.free("wdT" + sfx)
        ar.free("h" + sfx)
        ar.free("ys" + sfx)
        ar.free("r2" + sfx)
        ar.free("outb" + sfx)

    for s in range(SEQ_PER_CORE):
        emit_seq(s)

    stats = b.finalize_and_emit(final_wait_keys=["ost0", "ost1"])
    stats["sbuf_peak"] = ar.peak
    return nc, stats


def kernel(L0_attn_x, L0_attn_Wq, L0_attn_Wk, L0_attn_Wv, L0_attn_Wo,
           L0_ffn_w_gate, L0_ffn_w_up, L0_ffn_w_down):
    x = np.ascontiguousarray(np.asarray(L0_attn_x, dtype=np.float32))
    wq = np.ascontiguousarray(np.asarray(L0_attn_Wq, dtype=np.float32))
    wk = np.ascontiguousarray(np.asarray(L0_attn_Wk, dtype=np.float32))
    wv = np.ascontiguousarray(np.asarray(L0_attn_Wv, dtype=np.float32))
    wo = np.ascontiguousarray(np.asarray(L0_attn_Wo, dtype=np.float32))
    wg = np.ascontiguousarray(np.asarray(L0_ffn_w_gate, dtype=np.int32))
    wu = np.ascontiguousarray(np.asarray(L0_ffn_w_up, dtype=np.int32))
    wd = np.ascontiguousarray(np.asarray(L0_ffn_w_down, dtype=np.int32))

    nc, stats = build_program()
    in_maps = []
    for c in range(N_CORES):
        in_maps.append({
            "x": x[c * SEQ_PER_CORE:(c + 1) * SEQ_PER_CORE],
            "wq": wq, "wk": wk, "wv": wv, "wo": wo,
            "wg": wg, "wu": wu, "wd": wd,
        })
    res = run_bass_kernel_spmd(nc, in_maps, core_ids=list(range(N_CORES)))
    out = np.concatenate([r["out"] for r in res.results], axis=0)
    return out.astype(np.float32)


if __name__ == "__main__":
    nc, stats = build_program()
    print("build ok:", stats)


# version 05
# speedup vs baseline: 1.1550x; 1.1550x over previous
import numpy as np
import ml_dtypes
import concourse.bass as bass
import concourse.mybir as mybir
from concourse.bass_utils import run_bass_kernel_spmd

F32 = mybir.dt.float32
BF16 = mybir.dt.bfloat16
I32 = mybir.dt.int32
U8 = mybir.dt.uint8
AF = mybir.ActivationFunctionType
ALU = mybir.AluOpType
AX = mybir.AxisListType
NPBF = ml_dtypes.bfloat16

P = 128
S = 2048
D = 1024
H = 3072
NT = S // P
ND = D // P
NH = H // P
NG = NT // 4
SEQ_PER_CORE = 2
N_CORES = 8
INV_SQRT_DK = 1.0 / 32.0
DEQ = 1.0 / 127.0
EPS = 1e-5

BLOCK_METHOD = {"pe": "tensor", "act": "scalar", "dve": "vector", "pool": "gpsimd", "sp": "sync"}

DEBUG = False


class _Op:
    __slots__ = ("i", "eng", "fn", "deps", "name", "flag", "dma_key", "sem_id", "val", "know")

    def __init__(self, i, eng, fn, deps, name, dma_key):
        self.i, self.eng, self.fn, self.deps, self.name, self.dma_key = i, eng, fn, deps, name, dma_key
        self.flag = False
        self.sem_id = None
        self.val = None
        self.know = None


class Builder:

    def __init__(self, nc):
        self.nc = nc
        self.ops = []
        self.lw = {}
        self.rd = {}
        self.name_keys = {}
        self.alias_deps = {}

    @staticmethod
    def _key_name(k):
        return k[0] if isinstance(k, tuple) else k

    def op(self, eng, fn, reads=(), writes=(), name="", dma_key=None):
        i = len(self.ops)
        deps = set()
        for r in reads:
            if r in self.lw:
                deps.add(self.lw[r])
        for w in writes:
            if w in self.lw:
                deps.add(self.lw[w])
            for r_ in self.rd.get(w, ()):
                deps.add(r_)
            ad = self.alias_deps.get(self._key_name(w))
            if ad:
                deps |= ad
        deps.discard(i)
        o = _Op(i, eng, fn, deps, name, dma_key)
        self.ops.append(o)
        for r in reads:
            self.rd.setdefault(r, []).append(i)
            self.name_keys.setdefault(self._key_name(r), set()).add(r)
        for w in writes:
            self.lw[w] = i
            self.rd[w] = []
            self.name_keys.setdefault(self._key_name(w), set()).add(w)
        return i

    def dma(self, queue, fn, reads=(), writes=(), key=None, name=""):
        if key is None:
            key = "dq_" + queue
        return self.op(queue, fn, reads, writes, name, dma_key=key)

    _fence_fns = {}

    def set_fence_fns(self, fns):
        self._fence_fns = fns

    def fence(self, eng, keys):
        return self.op(eng, self._fence_fns[eng], reads=list(keys), writes=list(keys), name="fence")

    def pending_ops_of(self, tensor_name):
        out = set()
        for k in self.name_keys.get(tensor_name, ()):
            if k in self.lw:
                out.add(self.lw[k])
            out.update(self.rd.get(k, ()))
        return out

    def add_alias_deps(self, tensor_name, dep_ops):
        if dep_ops:
            self.alias_deps.setdefault(tensor_name, set()).update(dep_ops)

    def finalize_and_emit(self, final_wait_keys=()):
        nc = self.nc
        ops = self.ops
        for o in ops:
            for d in o.deps:
                p = ops[d]
                if p.dma_key is None and not (p.eng == o.eng and p.eng in ("pe", "sp")):
                    p.flag = True
        eng_sems = {}
        dma_sems = {}
        for o in ops:
            if o.dma_key is not None:
                if o.dma_key not in dma_sems:
                    dma_sems[o.dma_key] = nc.alloc_semaphore(name=("sd_" + str(o.dma_key))[:30])
            elif o.flag and o.eng not in eng_sems:
                eng_sems[o.eng] = nc.alloc_semaphore(name="se_" + o.eng)
        eng_counts = {}
        dma_counts = {}
        for o in ops:
            if o.dma_key is not None:
                c = dma_counts.get(o.dma_key, 0) + 16
                dma_counts[o.dma_key] = c
                o.sem_id = ("d", o.dma_key)
                o.val = c
            elif o.flag:
                c = eng_counts.get(o.eng, 0) + 1
                eng_counts[o.eng] = c
                o.sem_id = ("e", o.eng)
                o.val = c
        last_on_eng = {}
        for o in ops:
            know = dict(last_on_eng.get(o.eng, {}))
            for d in sorted(o.deps):
                p = ops[d]
                if p.sem_id is None:
                    continue
                if know.get(p.sem_id, 0) < p.val:
                    know[p.sem_id] = p.val
                for k, v in p.know.items():
                    if know.get(k, 0) < v:
                        know[k] = v
            o.know = know
            last_on_eng[o.eng] = know

        ops_by_eng = {}
        for o in ops:
            ops_by_eng.setdefault(o.eng, []).append(o)
        n_waits = [0]

        def emit_for(eng_key):
            def body(eng):
                know = {}
                for o in ops_by_eng[eng_key]:
                    for d in sorted(o.deps):
                        p = ops[d]
                        if p.sem_id is None:
                            continue
                        if know.get(p.sem_id, 0) >= p.val:
                            continue
                        sem = dma_sems[p.sem_id[1]] if p.sem_id[0] == "d" else eng_sems[p.sem_id[1]]
                        eng.wait_ge(sem, p.val)
                        n_waits[0] += 1
                        know[p.sem_id] = p.val
                        for k, v in p.know.items():
                            if know.get(k, 0) < v:
                                know[k] = v
                    ins = o.fn(eng)
                    if o.sem_id is not None:
                        ins.then_inc(dma_sems[o.sem_id[1]] if o.sem_id[0] == "d" else eng_sems[o.sem_id[1]],
                                     16 if o.sem_id[0] == "d" else 1)
                    for k, v in o.know.items():
                        if know.get(k, 0) < v:
                            know[k] = v
                if eng_key == "sp":
                    for k in final_wait_keys:
                        if k in dma_counts and know.get(("d", k), 0) < dma_counts[k]:
                            eng.wait_ge(dma_sems[k], dma_counts[k])
            return body

        with nc.Block() as block:
            keys = set(ops_by_eng) | ({"sp"} if final_wait_keys else set())
            for eng_key in keys:
                ops_by_eng.setdefault(eng_key, [])
                getattr(block, BLOCK_METHOD[eng_key])(emit_for(eng_key))
        return {"n_ops": len(ops), "n_waits": n_waits[0],
                "n_by_eng": {k: len(v) for k, v in ops_by_eng.items()}}


class Arena:

    def __init__(self, nc, b):
        self.nc = nc
        self.b = b
        avail = nc.sbuf_bytes_remaining - 2048
        self.backing = nc.alloc_sbuf_tensor("arena", [P, avail], U8)
        self.base = nc.lookup_mloc(self.backing).addr
        self.size = avail
        self.free_list = [[0, avail, set()]]
        self.live = {}
        self.peak = 0

    def alloc(self, name, shape, dtype):
        nbytes = int(np.prod(shape[1:])) * mybir.dt.size(dtype)
        nbytes = (nbytes + 31) // 32 * 32
        for ent in sorted(self.free_list, key=lambda e: e[1]):
            if ent[1] >= nbytes:
                off = ent[0]
                deps = set(ent[2])
                ent[0] += nbytes
                ent[1] -= nbytes
                if ent[1] == 0:
                    self.free_list.remove(ent)
                for nm, (o2, n2) in self.live.items():
                    assert off + nbytes <= o2 or o2 + n2 <= off, \
                        f"arena overlap: {name}@({off},{nbytes}) vs {nm}@({o2},{n2})"
                h = self.nc.alloc_sbuf_tensor_at(name, list(shape), dtype, offset=self.base + off)
                self.live[name] = (off, nbytes)
                self.peak = max(self.peak, max((o + n for o, n in self.live.values())))
                self.b.add_alias_deps(name, deps)
                return h
        raise MemoryError(f"arena: no space for {name} ({nbytes}B/part); "
                          f"free={[(e[0], e[1]) for e in self.free_list]}")

    def free(self, name):
        off, nbytes = self.live.pop(name)
        deps = self.b.pending_ops_of(name)
        self.free_list.append([off, nbytes, deps])
        self.free_list.sort(key=lambda e: e[0])
        merged = [self.free_list[0]]
        for ent in self.free_list[1:]:
            last = merged[-1]
            if last[0] + last[1] == ent[0]:
                last[1] += ent[1]
                last[2] = last[2] | ent[2]
            else:
                merged.append(ent)
        self.free_list = merged


def build_program():
    nc = bass.Bass("TRN2", target_bir_lowering=False)

    x_d = nc.dram_tensor("x", [SEQ_PER_CORE, S, D], F32, kind="ExternalInput")
    wq_d = nc.dram_tensor("wq", [D, D], F32, kind="ExternalInput")
    wk_d = nc.dram_tensor("wk", [D, D], F32, kind="ExternalInput")
    wv_d = nc.dram_tensor("wv", [D, D], F32, kind="ExternalInput")
    wo_d = nc.dram_tensor("wo", [D, D], F32, kind="ExternalInput")
    wg_d = nc.dram_tensor("wg", [H, D], I32, kind="ExternalInput")
    wu_d = nc.dram_tensor("wu", [H, D], I32, kind="ExternalInput")
    wd_d = nc.dram_tensor("wd", [D, H], I32, kind="ExternalInput")
    out_d = nc.dram_tensor("out", [SEQ_PER_CORE, S, D], F32, kind="ExternalOutput")
    y_dram = nc.dram_tensor("ydram", [S, D], F32, kind="ExternalOutput" if DEBUG else "Internal")
    dbg = {}
    if DEBUG:
        dbg["en"] = nc.dram_tensor("dbg_en", [NT, P, S], BF16, kind="ExternalOutput")
        dbg["rsum"] = nc.dram_tensor("dbg_rsum", [NT, P, 1], F32, kind="ExternalOutput")
        dbg["ao"] = nc.dram_tensor("dbg_ao", [ND, P, S], BF16, kind="ExternalOutput")
        dbg["r1"] = nc.dram_tensor("dbg_r1", [NT, P, D], F32, kind="ExternalOutput")
        dbg["scores"] = nc.dram_tensor("dbg_scores", [NT, P, S], F32, kind="ExternalOutput")
        dbg["ln"] = nc.dram_tensor("dbg_ln", [NT, P, 8], F32, kind="ExternalOutput")
        dbg["sq"] = nc.dram_tensor("dbg_sq", [NT, P, D], F32, kind="ExternalOutput")

    PF = nc.alloc_psum_tensor("PF", [P, 3072], F32)
    PB = nc.alloc_psum_tensor("PB", [P, 2048], BF16)
    assert nc.lookup_mloc(PF).bank == 0 and nc.lookup_mloc(PB).bank == 6

    def pf(col0, width):
        ap = PF[:, col0:col0 + width]
        keys = tuple(f"F{bk}" for bk in range(col0 // 512, (col0 + width - 1) // 512 + 1))
        return ap, keys

    def pbank(half, col0, width):
        ap = PB[:, half * 1024 + col0: half * 1024 + col0 + width]
        return ap, (f"B{6 + half}",)

    b = Builder(nc)
    ar = Arena(nc, b)

    identf = ar.alloc("identf", [P, P], F32)
    identb = ar.alloc("identb", [P, P], BF16)
    maskt = ar.alloc("maskt", [P, 4, 512], F32)
    nmax = ar.alloc("nmax", [P, 2], F32)
    rs_all = ar.alloc("rs_all", [P, NT], F32)
    ri_all = ar.alloc("ri_all", [P, NT], F32)
    s1 = ar.alloc("s1", [P, 2], F32)
    s2 = ar.alloc("s2", [P, 2], F32)
    negmean = ar.alloc("negmean", [P, 2], F32)
    nmean2 = ar.alloc("nmean2", [P, 2], F32)
    var2 = ar.alloc("var2", [P, 2], F32)
    lnl = ar.alloc("lnl", [P, 2], F32)
    rstd = ar.alloc("rstd", [P, 2], F32)
    sqbuf = ar.alloc("sqbuf", [P, D], F32)
    epsb = ar.alloc("epsb", [P, 1], F32)
    xs = ar.alloc("xs", [P, 4, D], F32)
    b.op("pool", lambda e: e.memset(epsb[:], EPS), writes=[("epsb",)])
    fb = ar.alloc("fb", [P, 6], F32)
    b.set_fence_fns({
        "dve": lambda e: e.memset(fb[:, 1:2], 0.0),
        "act": lambda e: e.memzero(fb[:, 3:4]),
        "pool": lambda e: e.memset(fb[:, 5:6], 0.0),
    })

    b.op("pool", lambda e: e.memset(identf[:], 0.0), writes=[("identf",)])
    b.op("pool", lambda e: e.affine_select(out=identf[:], in_=identf[:], compare_op=ALU.not_equal,
                                           fill=1.0, base=0, pattern=[[-1, P]], channel_multiplier=1),
         reads=[("identf",)], writes=[("identf",)])
    b.op("pool", lambda e: e.tensor_copy(identb[:], identf[:]), reads=[("identf",)], writes=[("identb",)])
    for j in range(4):
        b.op("pool", lambda e, j=j: e.memset(maskt[:, j, :], 0.0), writes=[("maskt", j)])
        b.op("pool", lambda e, j=j: e.affine_select(out=maskt[:, j, :], in_=maskt[:, j, :],
                                                    compare_op=ALU.is_ge, fill=-1e30, base=j * P,
                                                    pattern=[[-1, 512]], channel_multiplier=1),
             reads=[("maskt", j)], writes=[("maskt", j)])

    def emit_ln(src_ap, src_keys, dst_ap, dst_keys, sl, stats_eng, tap_t=None):
        def tap(col, key):
            if tap_t is None:
                return
            src = {"s1": s1, "s2": s2, "negmean": negmean, "nmean2": nmean2,
                   "var2": var2, "lnl": lnl, "rstd": rstd}[key]
            b.dma("sp", lambda e: e.dma_start(out=dbg["ln"].ap()[tap_t, :, col:col + 1], in_=src[:, sl:sl + 1],
                                              allow_slow_non_contiguous=True),
                  reads=[(key, sl)], key="dbg")

        b.op(stats_eng, lambda e: e.reduce_sum(s1[:, sl:sl + 1], src_ap, axis=AX.X),
             reads=src_keys, writes=[("s1", sl)])
        b.op(stats_eng, lambda e: e.scalar_tensor_tensor(sqbuf[:], src_ap, 1.0, src_ap, ALU.mult, ALU.mult,
                                                         accum_out=s2[:, sl:sl + 1]),
             reads=src_keys, writes=[("sqbuf",), ("s2", sl)])
        tap(0, "s1")
        tap(1, "s2")
        if tap_t is not None:
            b.dma("sp", lambda e: e.dma_start(out=dbg["sq"].ap()[tap_t], in_=sqbuf[:]),
                  reads=[("sqbuf",)], key="dbg")
        b.op("dve", lambda e: e.tensor_scalar_mul(negmean[:, sl:sl + 1], s1[:, sl:sl + 1], -1.0 / D),
             reads=[("s1", sl)], writes=[("negmean", sl)])
        b.op("dve", lambda e: e.tensor_scalar(nmean2[:, sl:sl + 1], negmean[:, sl:sl + 1],
                                              negmean[:, sl:sl + 1], -1.0, ALU.mult, ALU.mult),
             reads=[("negmean", sl)], writes=[("nmean2", sl)])
        b.op("dve", lambda e: e.tensor_scalar(var2[:, sl:sl + 1], s2[:, sl:sl + 1],
                                              1.0 / D, nmean2[:, sl:sl + 1], ALU.mult, ALU.add),
             reads=[("s2", sl), ("nmean2", sl)], writes=[("var2", sl)])
        b.op("act", lambda e: e.activation(lnl[:, sl:sl + 1], var2[:, sl:sl + 1], AF.Ln, bias=epsb[:], scale=1.0),
             reads=[("var2", sl), ("epsb",)], writes=[("lnl", sl)])
        b.op("act", lambda e: e.activation(rstd[:, sl:sl + 1], lnl[:, sl:sl + 1], AF.Exp, scale=-0.5),
             reads=[("lnl", sl)], writes=[("rstd", sl)])
        tap(2, "negmean")
        tap(3, "nmean2")
        tap(4, "var2")
        tap(5, "lnl")
        tap(6, "rstd")
        b.op("dve", lambda e: e.tensor_scalar(dst_ap, src_ap, negmean[:, sl:sl + 1], rstd[:, sl:sl + 1],
                                              ALU.add, ALU.mult),
             reads=list(src_keys) + [("negmean", sl), ("rstd", sl)], writes=dst_keys)

    def emit_seq(s):
        sfx = f"_s{s}"

        xT = ar.alloc("xT" + sfx, [P, ND, S], BF16)
        wqb = ar.alloc("wqb" + sfx, [P, ND, D], BF16)
        wkb = ar.alloc("wkb" + sfx, [P, ND, D], BF16)

        b.dma("pool", lambda e: e.dma_start(out=wqb[:], in_=wq_d.ap().rearrange("(ec p) k -> p ec k", p=P)),
              writes=[("wqb" + sfx,)], key="wq")
        b.dma("pool", lambda e: e.dma_start(out=wkb[:], in_=wk_d.ap().rearrange("(ec p) k -> p ec k", p=P)),
              writes=[("wkb" + sfx,)], key="wk")

        for t in range(NT):
            sl = t % 4
            b.dma("sp", lambda e, t=t, sl=sl: e.dma_start(out=xs[:, sl, :], in_=x_d.ap()[s, t * P:(t + 1) * P, :]),
                  writes=[("xs", sl)], key=f"xs{sl}")
            bk0 = 2 if t % 2 == 0 else 4
            for c in range(ND):
                ap, kk = pf((bk0 + c // 4) * 512 + (c % 4) * P, P)
                b.op("pe", lambda e, ap=ap, sl=sl, c=c: e.transpose(ap, xs[:, sl, c * P:(c + 1) * P], identf[:]),
                     reads=[("xs", sl), ("identf",)], writes=list(kk))
            for half in range(2):
                ap, kk = pf((bk0 + half) * 512, 512)
                dst = xT[:, half * 4:(half + 1) * 4, t * P:(t + 1) * P]
                b.op("act", lambda e, dst=dst, ap=ap: e.copy(dst, ap),
                     reads=list(kk), writes=[("xT" + sfx, half * 4 + c, t) for c in range(4)])

        def project_fmaj(wb, wname, dstname, dst):
            for m in range(ND):
                for half in range(2):
                    for ec in range(ND):
                        for n in (2 * half, 2 * half + 1):
                            ap, kk = pf((n % 4) * 512, 512)
                            b.op("pe", lambda e, ap=ap, m=m, ec=ec, n=n, wb=wb:
                                 e.matmul(ap, wb[:, ec, m * P:(m + 1) * P], xT[:, ec, n * 512:(n + 1) * 512],
                                          start=(ec == 0), stop=(ec == ND - 1)),
                                 reads=[(wname,)] + [("xT" + sfx, ec, 4 * n + i) for i in range(4)],
                                 writes=list(kk))
                    for n in (2 * half, 2 * half + 1):
                        ap, kk = pf((n % 4) * 512, 512)
                        b.op("act", lambda e, dst=dst, m=m, n=n, ap=ap: e.copy(dst[:, m, n * 512:(n + 1) * 512], ap),
                             reads=list(kk), writes=[(dstname, m, n)])

        qT = ar.alloc("qT" + sfx, [P, ND, S], BF16)
        kT = ar.alloc("kT" + sfx, [P, ND, S], BF16)
        project_fmaj(wqb, "wqb" + sfx, "qT" + sfx, qT)
        project_fmaj(wkb, "wkb" + sfx, "kT" + sfx, kT)
        ar.free("wqb" + sfx)
        ar.free("wkb" + sfx)

        wvb = ar.alloc("wvb" + sfx, [P, ND, D], BF16)
        b.dma("pool", lambda e: e.dma_start(out=wvb[:], in_=wv_d.ap().rearrange("(ec p) k -> p ec k", p=P)),
              writes=[("wvb" + sfx,)], key="wv")
        vtm = ar.alloc("v" + sfx, [P, NT, D], BF16)
        for t in range(NT):
            for ec in range(ND):
                for half in range(2):
                    ap, kk = pf(((t % 2) * 2 + half) * 512, 512)
                    b.op("pe", lambda e, ap=ap, ec=ec, t=t, half=half:
                         e.matmul(ap, xT[:, ec, t * P:(t + 1) * P], wvb[:, ec, half * 512:(half + 1) * 512],
                                  start=(ec == 0), stop=(ec == ND - 1)),
                         reads=[("wvb" + sfx,), ("xT" + sfx, ec, t)], writes=list(kk))
            for half in range(2):
                ap, kk = pf(((t % 2) * 2 + half) * 512, 512)
                b.op("act", lambda e, t=t, half=half, ap=ap: e.copy(vtm[:, t, half * 512:(half + 1) * 512], ap),
                     reads=list(kk), writes=[("v" + sfx, t)])
        ar.free("wvb" + sfx)
        ar.free("xT" + sfx)

        aoT = ar.alloc("aoT" + sfx, [P, ND, S], BF16)
        wob = ar.alloc("wob" + sfx, [P, ND, D], BF16)
        b.dma("pool", lambda e: e.dma_start(out=wob[:], in_=wo_d.ap().rearrange("(ec p) k -> p ec k", p=P)),
              writes=[("wob" + sfx,)], key="wo")
        scb = ar.alloc("sc" + sfx, [P, S], F32)
        Eb = ar.alloc("E" + sfx, [P, 2, S], BF16)
        ET = ar.alloc("ET" + sfx, [P, NT, 512], BF16)

        def emit_scores_softmax(t):
            g = t // 4
            wlen = (g + 1) * 512
            j = t % 4
            sl = t % 2
            for nb in range(g + 1):
                ap, kk = pf(nb * 512, 512)
                for ec in range(ND):
                    b.op("pe", lambda e, ap=ap, ec=ec, t=t, nb=nb:
                         e.matmul(ap, qT[:, ec, t * P:(t + 1) * P], kT[:, ec, nb * 512:(nb + 1) * 512],
                                  start=(ec == 0), stop=(ec == ND - 1)),
                         reads=[("qT" + sfx, ec, t // 4), ("kT" + sfx, ec, nb)], writes=list(kk))
                if nb == g:
                    b.op("dve", lambda e, ap=ap, nb=nb, j=j:
                         e.scalar_tensor_tensor(scb[:, nb * 512:(nb + 1) * 512], ap, INV_SQRT_DK,
                                                maskt[:, j, :], ALU.mult, ALU.add),
                         reads=list(kk) + [("maskt", j)], writes=[("sc" + sfx, nb)])
                else:
                    b.op("act", lambda e, ap=ap, nb=nb:
                         e.activation(scb[:, nb * 512:(nb + 1) * 512], ap, AF.Copy, scale=INV_SQRT_DK),
                         reads=list(kk), writes=[("sc" + sfx, nb)])
            sckeys = [("sc" + sfx, nb) for nb in range(g + 1)]
            b.op("dve", lambda e, wlen=wlen, sl=sl:
                 e.reduce_max(nmax[:, sl:sl + 1], scb[:, 0:wlen], axis=AX.X, negate=True),
                 reads=sckeys, writes=[("nmax", sl)])
            b.op("act", lambda e, wlen=wlen, sl=sl, t=t:
                 e.activation(Eb[:, sl, 0:wlen], scb[:, 0:wlen], AF.Exp, bias=nmax[:, sl:sl + 1], scale=1.0,
                              accum_out=rs_all[:, t:t + 1]),
                 reads=sckeys + [("nmax", sl)], writes=[("E" + sfx, sl), ("rs", t)])

        def emit_T(t):
            g = t // 4
            ncs = 4 * (g + 1)
            j = t % 4
            sl = t % 2
            for c0 in range(0, ncs, 8):
                cb = min(8, ncs - c0)
                half = (c0 // 8) % 2
                for ci in range(cb):
                    cs = c0 + ci
                    apb, kkb = pbank(half, ci * P, P)
                    b.op("pe", lambda e, apb=apb, cs=cs, sl=sl:
                         e.transpose(apb, Eb[:, sl, cs * P:(cs + 1) * P], identb[:]),
                         reads=[("E" + sfx, sl), ("identb",)], writes=list(kkb))
                apb, kkb = pbank(half, 0, cb * P)
                dst = ET[:, c0:c0 + cb, j * P:(j + 1) * P]
                b.op("dve", lambda e, dst=dst, apb=apb: e.tensor_copy(dst, apb),
                     reads=list(kkb), writes=[("ET" + sfx, c0 + ci, j) for ci in range(cb)])

        def emit_AV(g):
            ncs = 4 * (g + 1)
            for vc in range(ND):
                ap, kk = pf((4 + vc % 2) * 512, 512)
                for cs in range(ncs):
                    b.op("pe", lambda e, ap=ap, cs=cs, vc=vc, ncs=ncs:
                         e.matmul(ap, vtm[:, cs, vc * P:(vc + 1) * P], ET[:, cs, :],
                                  start=(cs == 0), stop=(cs == ncs - 1)),
                         reads=[("v" + sfx, cs)] + [("ET" + sfx, cs, j) for j in range(4)],
                         writes=list(kk))
                b.op("dve", lambda e, ap=ap, vc=vc, g=g: e.tensor_copy(aoT[:, vc, g * 512:(g + 1) * 512], ap),
                     reads=list(kk), writes=[("aoT" + sfx, vc, g)])

        prevT = [None]
        for g in range(NG):
            for j in range(4):
                t = 4 * g + j
                emit_scores_softmax(t)
                if prevT[0] is not None:
                    pt = prevT[0]
                    prevT[0] = None
                    pt()
                if j == 0 and g > 0:
                    emit_AV(g - 1)
                prevT[0] = (lambda t=t: emit_T(t))
        prevT[0]()
        emit_AV(NG - 1)
        b.op("dve", lambda e: e.reciprocal(ri_all[:], rs_all[:]),
             reads=[("rs", t) for t in range(NT)], writes=[("ri",)])

        ar.free("qT" + sfx)
        ar.free("kT" + sfx)
        ar.free("v" + sfx)
        ar.free("sc" + sfx)
        ar.free("E" + sfx)
        ar.free("ET" + sfx)

        r1 = ar.alloc("r1" + sfx, [P, 2, D], F32)
        ybuf = ar.alloc("y" + sfx, [P, 2, D], F32)
        yT = ar.alloc("yT" + sfx, [P, ND, S], BF16)

        def emit_z_ln(t):
            sl = t % 2
            sl4 = t % 4
            bz = 0 if t % 2 == 0 else 2
            b.dma("sp", lambda e, t=t, sl4=sl4: e.dma_start(out=xs[:, sl4, :], in_=x_d.ap()[s, t * P:(t + 1) * P, :]),
                  writes=[("xs", sl4)], key=f"xs{sl4}")
            for half in range(2):
                for vc in range(ND):
                    ap, kk = pf((bz + half) * 512, 512)
                    b.op("pe", lambda e, ap=ap, vc=vc, t=t, half=half:
                         e.matmul(ap, aoT[:, vc, t * P:(t + 1) * P], wob[:, vc, half * 512:(half + 1) * 512],
                                  start=(vc == 0), stop=(vc == ND - 1)),
                         reads=[("wob" + sfx,), ("aoT" + sfx, vc, t // 4)], writes=list(kk))
            apz, kkz = pf(bz * 512, 1024)
            b.op("dve", lambda e, apz=apz, sl=sl, sl4=sl4, t=t:
                 e.scalar_tensor_tensor(r1[:, sl, :], apz, ri_all[:, t:t + 1], xs[:, sl4, :], ALU.mult, ALU.add),
                 reads=list(kkz) + [("xs", sl4), ("ri",)], writes=[("r1" + sfx, sl)])
            emit_ln(r1[:, sl, :], [("r1" + sfx, sl)], ybuf[:, sl, :], [("y" + sfx, sl)], sl, "dve")
            b.dma("sp", lambda e, t=t, sl=sl: e.dma_start(out=y_dram.ap()[t * P:(t + 1) * P, :], in_=ybuf[:, sl, :]),
                  reads=[("y" + sfx, sl)], writes=[("ydram", t)], key=f"yst{sl}")

        def emit_yT(t):
            sl = t % 2
            for c in range(ND):
                ap, kk = pf((4 + c // 4) * 512 + (c % 4) * P, P)
                b.op("pe", lambda e, ap=ap, sl=sl, c=c: e.transpose(ap, ybuf[:, sl, c * P:(c + 1) * P], identf[:]),
                     reads=[("y" + sfx, sl), ("identf",)], writes=list(kk))
            for half in range(2):
                ap, kk = pf((4 + half) * 512, 512)
                dst = yT[:, half * 4:(half + 1) * 4, t * P:(t + 1) * P]
                b.op("act", lambda e, dst=dst, ap=ap: e.copy(dst, ap),
                     reads=list(kk), writes=[("yT" + sfx, half * 4 + c, t) for c in range(4)])

        prevY = [None]
        for t in range(NT):
            emit_z_ln(t)
            if prevY[0] is not None:
                py = prevY[0]
                prevY[0] = None
                py()
            prevY[0] = (lambda t=t: emit_yT(t))
        prevY[0]()

        ar.free("aoT" + sfx)
        ar.free("wob" + sfx)
        ar.free("r1" + sfx)
        ar.free("y" + sfx)

        h_actT = ar.alloc("h" + sfx, [P, NH, S], BF16)
        wgb = ar.alloc("wgb" + sfx, [P, 2, D], BF16)
        wub = ar.alloc("wub" + sfx, [P, 2, D], BF16)
        wgT = ar.alloc("wgT" + sfx, [P, 2, D], BF16)
        wuT = ar.alloc("wuT" + sfx, [P, 2, D], BF16)
        sg = ar.alloc("sg" + sfx, [P, 2, 512], BF16)

        for hc in range(NH):
            sl = hc % 2
            b.dma("pool", lambda e, hc=hc, sl=sl: e.dma_start(out=wgb[:, sl, :], in_=wg_d.ap()[hc * P:(hc + 1) * P, :]),
                  writes=[("wgb" + sfx, sl)], key=f"wg{sl}")
            b.dma("pool", lambda e, hc=hc, sl=sl: e.dma_start(out=wub[:, sl, :], in_=wu_d.ap()[hc * P:(hc + 1) * P, :]),
                  writes=[("wub" + sfx, sl)], key=f"wu{sl}")
            for (wsrc, wsname, wt, wtname, half) in ((wgb, "wgb", wgT, "wgT", 0), (wub, "wub", wuT, "wuT", 1)):
                for dc in range(ND):
                    apb, kkb = pbank(half, dc * P, P)
                    b.op("pe", lambda e, apb=apb, wsrc=wsrc, wsname=wsname, sl=sl, dc=dc:
                         e.transpose(apb, wsrc[:, sl, dc * P:(dc + 1) * P], identb[:]),
                         reads=[(wsname + sfx, sl), ("identb",)], writes=list(kkb))
                apb, kkb = pbank(half, 0, D)
                b.op("dve", lambda e, wt=wt, sl=sl, apb=apb: e.tensor_copy(wt[:, sl, :], apb),
                     reads=list(kkb), writes=[(wtname + sfx, sl)])
            for n in range(4):
                for dc in range(ND):
                    apg, kkg = pf((n % 2) * 512, 512)
                    b.op("pe", lambda e, apg=apg, dc=dc, n=n, sl=sl:
                         e.matmul(apg, wgT[:, sl, dc * P:(dc + 1) * P], yT[:, dc, n * 512:(n + 1) * 512],
                                  start=(dc == 0), stop=(dc == ND - 1)),
                         reads=[("wgT" + sfx, sl)] + [("yT" + sfx, dc, 4 * n + i) for i in range(4)],
                         writes=list(kkg))
                for dc in range(ND):
                    apu, kku = pf((2 + n % 2) * 512, 512)
                    b.op("pe", lambda e, apu=apu, dc=dc, n=n, sl=sl:
                         e.matmul(apu, wuT[:, sl, dc * P:(dc + 1) * P], yT[:, dc, n * 512:(n + 1) * 512],
                                  start=(dc == 0), stop=(dc == ND - 1)),
                         reads=[("wuT" + sfx, sl)] + [("yT" + sfx, dc, 4 * n + i) for i in range(4)],
                         writes=list(kku))
                apg, kkg = pf((n % 2) * 512, 512)
                apu, kku = pf((2 + n % 2) * 512, 512)
                b.op("act", lambda e, apg=apg, n=n: e.activation(sg[:, n % 2, :], apg, AF.Silu, scale=DEQ),
                     reads=list(kkg), writes=[("sg" + sfx, n % 2)])
                b.op("dve", lambda e, apu=apu, n=n, hc=hc:
                     e.tensor_tensor(h_actT[:, hc, n * 512:(n + 1) * 512], apu, sg[:, n % 2, :], ALU.mult),
                     reads=list(kku) + [("sg" + sfx, n % 2)], writes=[("h" + sfx, hc, n)])

        ar.free("yT" + sfx)
        ar.free("wgb" + sfx)
        ar.free("wub" + sfx)
        ar.free("wgT" + sfx)
        ar.free("wuT" + sfx)
        ar.free("sg" + sfx)

        wdb = ar.alloc("wdb" + sfx, [P, 2, ND, 256], BF16)
        wdT = ar.alloc("wdT" + sfx, [P, NH, D], BF16)
        ys = ar.alloc("ys" + sfx, [P, 2, D], F32)
        r2 = ar.alloc("r2" + sfx, [P, 2, D], F32)
        outb = ar.alloc("outb" + sfx, [P, 2, D], F32)

        for grp in range(NH // 2):
            slw = grp % 2
            src = wd_d.ap()[:, grp * 256:(grp + 1) * 256].rearrange("(dc p) h -> p dc h", p=P)
            b.dma("pool", lambda e, src=src, slw=slw: e.dma_start(out=wdb[:, slw], in_=src),
                  writes=[("wdb" + sfx, slw)], key=f"wd{slw}")
            for hl in range(2):
                hc = grp * 2 + hl
                half = hc % 2
                for dc in range(ND):
                    apb, kkb = pbank(half, dc * P, P)
                    b.op("pe", lambda e, apb=apb, slw=slw, dc=dc, hl=hl:
                         e.transpose(apb, wdb[:, slw, dc, hl * P:(hl + 1) * P], identb[:]),
                         reads=[("wdb" + sfx, slw), ("identb",)], writes=list(kkb))
                apb, kkb = pbank(half, 0, D)
                b.op("dve", lambda e, apb=apb, hc=hc: e.tensor_copy(wdT[:, hc, :], apb),
                     reads=list(kkb), writes=[("wdT" + sfx, hc)])

        for t in range(NT):
            sl = t % 2
            bz = 0 if t % 2 == 0 else 2
            b.dma("sp", lambda e, t=t, sl=sl: e.dma_start(out=ys[:, sl, :], in_=y_dram.ap()[t * P:(t + 1) * P, :]),
                  reads=[("ydram", t)], writes=[("ys" + sfx, sl)], key=f"ys{sl}")
            for hc in range(NH):
                for half in range(2):
                    ap, kk = pf((bz + half) * 512, 512)
                    b.op("pe", lambda e, ap=ap, hc=hc, t=t, half=half:
                         e.matmul(ap, h_actT[:, hc, t * P:(t + 1) * P], wdT[:, hc, half * 512:(half + 1) * 512],
                                  start=(hc == 0), stop=(hc == NH - 1)),
                         reads=[("h" + sfx, hc, t // 4), ("wdT" + sfx, hc)], writes=list(kk))
            apz, kkz = pf(bz * 512, 1024)
            b.op("dve", lambda e, apz=apz, sl=sl:
                 e.scalar_tensor_tensor(r2[:, sl, :], apz, DEQ * DEQ, ys[:, sl, :], ALU.mult, ALU.add),
                 reads=list(kkz) + [("ys" + sfx, sl)], writes=[("r2" + sfx, sl)])
            emit_ln(r2[:, sl, :], [("r2" + sfx, sl)], outb[:, sl, :], [("outb" + sfx, sl)], sl, "dve")
            b.dma("sp", lambda e, t=t, sl=sl: e.dma_start(out=out_d.ap()[s, t * P:(t + 1) * P, :], in_=outb[:, sl, :]),
                  reads=[("outb" + sfx, sl)], key=f"ost{sl}")

        ar.free("wdb" + sfx)
        ar.free("wdT" + sfx)
        ar.free("h" + sfx)
        ar.free("ys" + sfx)
        ar.free("r2" + sfx)
        ar.free("outb" + sfx)

    for s in range(SEQ_PER_CORE):
        emit_seq(s)

    stats = b.finalize_and_emit(final_wait_keys=["ost0", "ost1"])
    stats["sbuf_peak"] = ar.peak
    return nc, stats


def kernel(L0_attn_x, L0_attn_Wq, L0_attn_Wk, L0_attn_Wv, L0_attn_Wo,
           L0_ffn_w_gate, L0_ffn_w_up, L0_ffn_w_down):
    x = np.ascontiguousarray(np.asarray(L0_attn_x, dtype=np.float32))
    wq = np.ascontiguousarray(np.asarray(L0_attn_Wq, dtype=np.float32))
    wk = np.ascontiguousarray(np.asarray(L0_attn_Wk, dtype=np.float32))
    wv = np.ascontiguousarray(np.asarray(L0_attn_Wv, dtype=np.float32))
    wo = np.ascontiguousarray(np.asarray(L0_attn_Wo, dtype=np.float32))
    wg = np.ascontiguousarray(np.asarray(L0_ffn_w_gate, dtype=np.int32))
    wu = np.ascontiguousarray(np.asarray(L0_ffn_w_up, dtype=np.int32))
    wd = np.ascontiguousarray(np.asarray(L0_ffn_w_down, dtype=np.int32))

    nc, stats = build_program()
    in_maps = []
    for c in range(N_CORES):
        in_maps.append({
            "x": x[c * SEQ_PER_CORE:(c + 1) * SEQ_PER_CORE],
            "wq": wq, "wk": wk, "wv": wv, "wo": wo,
            "wg": wg, "wu": wu, "wd": wd,
        })
    res = run_bass_kernel_spmd(nc, in_maps, core_ids=list(range(N_CORES)))
    out = np.concatenate([r["out"] for r in res.results], axis=0)
    return out.astype(np.float32)


if __name__ == "__main__":
    nc, stats = build_program()
    print("build ok:", stats)


# version 06
# speedup vs baseline: 1.7184x; 1.4879x over previous
import bisect
import numpy as np
import ml_dtypes
import concourse.bass as bass
import concourse.mybir as mybir
from concourse.bass_utils import run_bass_kernel_spmd

F32 = mybir.dt.float32
BF16 = mybir.dt.bfloat16
I32 = mybir.dt.int32
U8 = mybir.dt.uint8
AF = mybir.ActivationFunctionType
ALU = mybir.AluOpType
AX = mybir.AxisListType
NPBF = ml_dtypes.bfloat16

P = 128
S = 2048
D = 1024
H = 3072
NT = S // P
ND = D // P
NH = H // P
NG = NT // 4
SEQ_PER_CORE = 2
N_CORES = 8
INV_SQRT_DK = 1.0 / 32.0
DEQ = 1.0 / 127.0
EPS = 1e-5

BLOCK_METHOD = {"pe": "tensor", "act": "scalar", "dve": "vector", "pool": "gpsimd", "sp": "sync"}


class _Op:
    __slots__ = ("i", "eng", "fn", "deps", "name", "flag", "dma_key", "sem_id", "val", "know")

    def __init__(self, i, eng, fn, deps, name, dma_key):
        self.i, self.eng, self.fn, self.deps, self.name, self.dma_key = i, eng, fn, deps, name, dma_key
        self.flag = False
        self.sem_id = None
        self.val = None
        self.know = None


class Builder:

    def __init__(self, nc):
        self.nc = nc
        self.ops = []
        self.lw = {}
        self.rd = {}
        self.name_keys = {}
        self.alias_deps = {}
        self.coarse_keys = set()

    @staticmethod
    def _key_name(k):
        return k[0] if isinstance(k, tuple) else k

    def op(self, eng, fn, reads=(), writes=(), name="", dma_key=None):
        i = len(self.ops)
        deps = set()
        for r in reads:
            if r in self.lw:
                deps.add(self.lw[r])
        for w in writes:
            if w in self.lw:
                deps.add(self.lw[w])
            for r_ in self.rd.get(w, ()):
                deps.add(r_)
            ad = self.alias_deps.get(self._key_name(w))
            if ad:
                deps |= ad
        deps.discard(i)
        o = _Op(i, eng, fn, deps, name, dma_key)
        self.ops.append(o)
        for r in reads:
            self.rd.setdefault(r, []).append(i)
            self.name_keys.setdefault(self._key_name(r), set()).add(r)
        for w in writes:
            self.lw[w] = i
            self.rd[w] = []
            self.name_keys.setdefault(self._key_name(w), set()).add(w)
        return i

    def dma(self, queue, fn, reads=(), writes=(), key=None, name="", coarse=False):
        if key is None:
            key = "dq_" + queue
        if coarse:
            self.coarse_keys.add(key)
        return self.op(queue, fn, reads, writes, name, dma_key=key)

    def pending_ops_of(self, tensor_name):
        out = set()
        for k in self.name_keys.get(tensor_name, ()):
            if k in self.lw:
                out.add(self.lw[k])
            out.update(self.rd.get(k, ()))
        return out

    def add_alias_deps(self, tensor_name, dep_ops):
        if dep_ops:
            self.alias_deps.setdefault(tensor_name, set()).update(dep_ops)

    def finalize_and_emit(self, final_wait_keys=()):
        nc = self.nc
        ops = self.ops
        for o in ops:
            for d in o.deps:
                p = ops[d]
                if p.dma_key is None and not (p.eng == o.eng and p.eng in ("pe", "sp")):
                    p.flag = True
        eng_sems = {}
        dma_sems = {}
        for o in ops:
            if o.dma_key is not None:
                if o.dma_key not in dma_sems:
                    dma_sems[o.dma_key] = nc.alloc_semaphore(name=("sd_" + str(o.dma_key))[:30])
            elif o.flag and o.eng not in eng_sems:
                eng_sems[o.eng] = nc.alloc_semaphore(name="se_" + o.eng)
        eng_counts = {}
        dma_counts = {}
        for o in ops:
            if o.dma_key is not None:
                c = dma_counts.get(o.dma_key, 0) + 16
                dma_counts[o.dma_key] = c
                o.sem_id = ("d", o.dma_key)
                o.val = c
            elif o.flag:
                c = eng_counts.get(o.eng, 0) + 1
                eng_counts[o.eng] = c
                o.sem_id = ("e", o.eng)
                o.val = c

        key_idx = {}
        for o in ops:
            if o.dma_key in self.coarse_keys:
                key_idx.setdefault(o.dma_key, []).append(o.i)

        def eff_val(p, oi):
            if p.dma_key in self.coarse_keys:
                return 16 * bisect.bisect_left(key_idx[p.dma_key], oi)
            return p.val

        last_on_eng = {}
        for o in ops:
            know = dict(last_on_eng.get(o.eng, {}))
            for d in sorted(o.deps):
                p = ops[d]
                if p.sem_id is None:
                    continue
                v = eff_val(p, o.i)
                if know.get(p.sem_id, 0) < v:
                    know[p.sem_id] = v
                for k, v2 in p.know.items():
                    if know.get(k, 0) < v2:
                        know[k] = v2
            o.know = know
            last_on_eng[o.eng] = know

        ops_by_eng = {}
        for o in ops:
            ops_by_eng.setdefault(o.eng, []).append(o)
        n_waits = [0]

        def emit_for(eng_key):
            def body(eng):
                know = {}
                for o in ops_by_eng[eng_key]:
                    for d in sorted(o.deps):
                        p = ops[d]
                        if p.sem_id is None:
                            continue
                        v = eff_val(p, o.i)
                        if know.get(p.sem_id, 0) >= v:
                            continue
                        sem = dma_sems[p.sem_id[1]] if p.sem_id[0] == "d" else eng_sems[p.sem_id[1]]
                        eng.wait_ge(sem, v)
                        n_waits[0] += 1
                        know[p.sem_id] = v
                        for k, v2 in p.know.items():
                            if know.get(k, 0) < v2:
                                know[k] = v2
                    ins = o.fn(eng)
                    if o.sem_id is not None:
                        ins.then_inc(dma_sems[o.sem_id[1]] if o.sem_id[0] == "d" else eng_sems[o.sem_id[1]],
                                     16 if o.sem_id[0] == "d" else 1)
                    for k, v2 in o.know.items():
                        if know.get(k, 0) < v2:
                            know[k] = v2
                if eng_key == "sp":
                    for k in final_wait_keys:
                        if k in dma_counts and know.get(("d", k), 0) < dma_counts[k]:
                            eng.wait_ge(dma_sems[k], dma_counts[k])
            return body

        with nc.Block() as block:
            keys = set(ops_by_eng) | ({"sp"} if final_wait_keys else set())
            for eng_key in keys:
                ops_by_eng.setdefault(eng_key, [])
                getattr(block, BLOCK_METHOD[eng_key])(emit_for(eng_key))
        return {"n_ops": len(ops), "n_waits": n_waits[0],
                "n_by_eng": {k: len(v) for k, v in ops_by_eng.items()}}


class Arena:

    def __init__(self, nc, b):
        self.nc = nc
        self.b = b
        avail = nc.sbuf_bytes_remaining - 2048
        self.backing = nc.alloc_sbuf_tensor("arena", [P, avail], U8)
        self.base = nc.lookup_mloc(self.backing).addr
        self.size = avail
        self.free_list = [[0, avail, set()]]
        self.live = {}
        self.peak = 0

    def alloc(self, name, shape, dtype):
        nbytes = int(np.prod(shape[1:])) * mybir.dt.size(dtype)
        nbytes = (nbytes + 31) // 32 * 32
        for ent in sorted(self.free_list, key=lambda e: e[1]):
            if ent[1] >= nbytes:
                off = ent[0]
                deps = set(ent[2])
                ent[0] += nbytes
                ent[1] -= nbytes
                if ent[1] == 0:
                    self.free_list.remove(ent)
                for nm, (o2, n2) in self.live.items():
                    assert off + nbytes <= o2 or o2 + n2 <= off, \
                        f"arena overlap: {name}@({off},{nbytes}) vs {nm}@({o2},{n2})"
                h = self.nc.alloc_sbuf_tensor_at(name, list(shape), dtype, offset=self.base + off)
                self.live[name] = (off, nbytes)
                self.peak = max(self.peak, max(o + n for o, n in self.live.values()))
                self.b.add_alias_deps(name, deps)
                return h
        raise MemoryError(f"arena: no space for {name} ({nbytes}B/part); "
                          f"free={[(e[0], e[1]) for e in self.free_list]}")

    def free(self, name):
        off, nbytes = self.live.pop(name)
        deps = self.b.pending_ops_of(name)
        self.free_list.append([off, nbytes, deps])
        self.free_list.sort(key=lambda e: e[0])
        merged = [self.free_list[0]]
        for ent in self.free_list[1:]:
            last = merged[-1]
            if last[0] + last[1] == ent[0]:
                last[1] += ent[1]
                last[2] = last[2] | ent[2]
            else:
                merged.append(ent)
        self.free_list = merged


def build_program():
    nc = bass.Bass("TRN2", target_bir_lowering=False)

    x_d = nc.dram_tensor("x", [SEQ_PER_CORE, S, D], F32, kind="ExternalInput")
    wq_d = nc.dram_tensor("wq", [D, D], F32, kind="ExternalInput")
    wk_d = nc.dram_tensor("wk", [D, D], F32, kind="ExternalInput")
    wv_d = nc.dram_tensor("wv", [D, D], F32, kind="ExternalInput")
    wo_d = nc.dram_tensor("wo", [D, D], F32, kind="ExternalInput")
    wg_d = nc.dram_tensor("wg", [H, D], I32, kind="ExternalInput")
    wu_d = nc.dram_tensor("wu", [H, D], I32, kind="ExternalInput")
    wd_d = nc.dram_tensor("wd", [D, H], I32, kind="ExternalInput")
    out_d = nc.dram_tensor("out", [SEQ_PER_CORE, S, D], F32, kind="ExternalOutput")
    y_dram = nc.dram_tensor("ydram", [S, D], F32, kind="Internal")
    xbf_dram = nc.dram_tensor("xbf", [S, D], BF16, kind="Internal")
    ybf_dram = nc.dram_tensor("ybf", [S, D], BF16, kind="Internal")
    wgbf_d = nc.dram_tensor("wgbf", [H, D], BF16, kind="Internal")
    wubf_d = nc.dram_tensor("wubf", [H, D], BF16, kind="Internal")
    wdbf_d = nc.dram_tensor("wdbf", [D, H], BF16, kind="Internal")

    PF = nc.alloc_psum_tensor("PF", [P, 3072], F32)
    PB = nc.alloc_psum_tensor("PB", [P, 2048], BF16)
    assert nc.lookup_mloc(PF).bank == 0 and nc.lookup_mloc(PB).bank == 6

    def pf(col0, width):
        ap = PF[:, col0:col0 + width]
        keys = tuple(f"F{bk}" for bk in range(col0 // 512, (col0 + width - 1) // 512 + 1))
        return ap, keys

    def pbank(half, col0, width):
        ap = PB[:, half * 1024 + col0: half * 1024 + col0 + width]
        return ap, (f"B{6 + half}",)

    b = Builder(nc)
    ar = Arena(nc, b)

    identb = ar.alloc("identb", [P, P], BF16)
    maskt = ar.alloc("maskt", [P, 4, 512], F32)
    nmax = ar.alloc("nmax", [P, 2], F32)
    rs_all = ar.alloc("rs_all", [P, NT], F32)
    ri_all = ar.alloc("ri_all", [P, NT], F32)
    s1 = ar.alloc("s1", [P, 2], F32)
    s2 = ar.alloc("s2", [P, 2], F32)
    negmean = ar.alloc("negmean", [P, 2], F32)
    nmean2 = ar.alloc("nmean2", [P, 2], F32)
    var2 = ar.alloc("var2", [P, 2], F32)
    lnl = ar.alloc("lnl", [P, 2], F32)
    rstd = ar.alloc("rstd", [P, 2], F32)
    sqbuf = ar.alloc("sqbuf", [P, D], F32)
    epsb = ar.alloc("epsb", [P, 1], F32)
    xs = ar.alloc("xs", [P, 2, D], F32)
    b.op("pool", lambda e: e.memset(epsb[:], EPS), writes=[("epsb",)])
    b.op("pool", lambda e: e.memset(identb[:], 0.0), writes=[("identb",)])
    b.op("pool", lambda e: e.affine_select(out=identb[:], in_=identb[:], compare_op=ALU.not_equal,
                                           fill=1.0, base=0, pattern=[[-1, P]], channel_multiplier=1),
         reads=[("identb",)], writes=[("identb",)])
    for j in range(4):
        b.op("pool", lambda e, j=j: e.memset(maskt[:, j, :], 0.0), writes=[("maskt", j)])
        b.op("pool", lambda e, j=j: e.affine_select(out=maskt[:, j, :], in_=maskt[:, j, :],
                                                    compare_op=ALU.is_ge, fill=-1e30, base=j * P,
                                                    pattern=[[-1, 512]], channel_multiplier=1),
             reads=[("maskt", j)], writes=[("maskt", j)])

    def emit_ln(src_ap, src_keys, dst_ap, dst_keys, sl, stats_eng):
        b.op(stats_eng, lambda e: e.reduce_sum(s1[:, sl:sl + 1], src_ap, axis=AX.X),
             reads=src_keys, writes=[("s1", sl)])
        b.op(stats_eng, lambda e: e.scalar_tensor_tensor(sqbuf[:], src_ap, 1.0, src_ap, ALU.mult, ALU.mult,
                                                         accum_out=s2[:, sl:sl + 1]),
             reads=src_keys, writes=[("sqbuf",), ("s2", sl)])
        b.op("dve", lambda e: e.tensor_scalar_mul(negmean[:, sl:sl + 1], s1[:, sl:sl + 1], -1.0 / D),
             reads=[("s1", sl)], writes=[("negmean", sl)])
        b.op("dve", lambda e: e.tensor_scalar(nmean2[:, sl:sl + 1], negmean[:, sl:sl + 1],
                                              negmean[:, sl:sl + 1], -1.0, ALU.mult, ALU.mult),
             reads=[("negmean", sl)], writes=[("nmean2", sl)])
        b.op("dve", lambda e: e.tensor_scalar(var2[:, sl:sl + 1], s2[:, sl:sl + 1],
                                              1.0 / D, nmean2[:, sl:sl + 1], ALU.mult, ALU.add),
             reads=[("s2", sl), ("nmean2", sl)], writes=[("var2", sl)])
        b.op("act", lambda e: e.activation(lnl[:, sl:sl + 1], var2[:, sl:sl + 1], AF.Ln, bias=epsb[:], scale=1.0),
             reads=[("var2", sl), ("epsb",)], writes=[("lnl", sl)])
        b.op("act", lambda e: e.activation(rstd[:, sl:sl + 1], lnl[:, sl:sl + 1], AF.Exp, scale=-0.5),
             reads=[("lnl", sl)], writes=[("rstd", sl)])
        b.op("dve", lambda e: e.tensor_scalar(dst_ap, src_ap, negmean[:, sl:sl + 1], rstd[:, sl:sl + 1],
                                              ALU.add, ALU.mult),
             reads=list(src_keys) + [("negmean", sl), ("rstd", sl)], writes=dst_keys)

    def emit_seq(s):
        sfx = f"_s{s}"

        xT = ar.alloc("xT" + sfx, [P, ND, S], BF16)
        wqb = ar.alloc("wqb" + sfx, [P, ND, D], BF16)
        wkb = ar.alloc("wkb" + sfx, [P, ND, D], BF16)

        b.dma("pool", lambda e: e.dma_start(out=xbf_dram.ap(), in_=x_d.ap()[s]),
              writes=[("xbf",)], key="xbfc")
        b.dma("pool", lambda e: e.dma_start(out=wqb[:], in_=wq_d.ap().rearrange("(ec p) k -> p ec k", p=P)),
              writes=[("wqb" + sfx,)], key="wq")
        b.dma("pool", lambda e: e.dma_start(out=wkb[:], in_=wk_d.ap().rearrange("(ec p) k -> p ec k", p=P)),
              writes=[("wkb" + sfx,)], key="wk")
        for dc in range(ND):
            b.dma("sp", lambda e, dc=dc: e.dma_start_transpose(xT[:, dc, :], xbf_dram.ap()[:, dc * P:(dc + 1) * P]),
                  reads=[("xbf",)], writes=[("xT" + sfx, dc)], key="xTt", coarse=True)

        def project_fmaj(wb, wname, dstname, dst):
            for m in range(ND):
                for half in range(2):
                    for ec in range(ND):
                        for n in (2 * half, 2 * half + 1):
                            ap, kk = pf((n % 4) * 512, 512)
                            b.op("pe", lambda e, ap=ap, m=m, ec=ec, n=n, wb=wb:
                                 e.matmul(ap, wb[:, ec, m * P:(m + 1) * P], xT[:, ec, n * 512:(n + 1) * 512],
                                          start=(ec == 0), stop=(ec == ND - 1)),
                                 reads=[(wname,), ("xT" + sfx, ec)], writes=list(kk))
                    for n in (2 * half, 2 * half + 1):
                        ap, kk = pf((n % 4) * 512, 512)
                        b.op("act", lambda e, dst=dst, m=m, n=n, ap=ap: e.copy(dst[:, m, n * 512:(n + 1) * 512], ap),
                             reads=list(kk), writes=[(dstname, m, n)])

        qT = ar.alloc("qT" + sfx, [P, ND, S], BF16)
        kT = ar.alloc("kT" + sfx, [P, ND, S], BF16)
        project_fmaj(wqb, "wqb" + sfx, "qT" + sfx, qT)
        project_fmaj(wkb, "wkb" + sfx, "kT" + sfx, kT)
        ar.free("wqb" + sfx)
        ar.free("wkb" + sfx)

        wvb = ar.alloc("wvb" + sfx, [P, ND, D], BF16)
        b.dma("pool", lambda e: e.dma_start(out=wvb[:], in_=wv_d.ap().rearrange("(ec p) k -> p ec k", p=P)),
              writes=[("wvb" + sfx,)], key="wv")
        vtm = ar.alloc("v" + sfx, [P, NT, D], BF16)
        for t in range(NT):
            for ec in range(ND):
                for half in range(2):
                    ap, kk = pf(((t % 2) * 2 + half) * 512, 512)
                    b.op("pe", lambda e, ap=ap, ec=ec, t=t, half=half:
                         e.matmul(ap, xT[:, ec, t * P:(t + 1) * P], wvb[:, ec, half * 512:(half + 1) * 512],
                                  start=(ec == 0), stop=(ec == ND - 1)),
                         reads=[("wvb" + sfx,), ("xT" + sfx, ec)], writes=list(kk))
            for half in range(2):
                ap, kk = pf(((t % 2) * 2 + half) * 512, 512)
                b.op("act", lambda e, t=t, half=half, ap=ap: e.copy(vtm[:, t, half * 512:(half + 1) * 512], ap),
                     reads=list(kk), writes=[("v" + sfx, t)])
        ar.free("wvb" + sfx)
        ar.free("xT" + sfx)

        aoT = ar.alloc("aoT" + sfx, [P, ND, S], BF16)
        wob = ar.alloc("wob" + sfx, [P, ND, D], BF16)
        b.dma("pool", lambda e: e.dma_start(out=wob[:], in_=wo_d.ap().rearrange("(ec p) k -> p ec k", p=P)),
              writes=[("wob" + sfx,)], key="wo")
        if s == 0:
            b.dma("pool", lambda e: e.dma_start(out=wgbf_d.ap(), in_=wg_d.ap()), writes=[("wgbf",)], key="wgc")
            b.dma("pool", lambda e: e.dma_start(out=wubf_d.ap(), in_=wu_d.ap()), writes=[("wubf",)], key="wuc")
            b.dma("pool", lambda e: e.dma_start(out=wdbf_d.ap(), in_=wd_d.ap()), writes=[("wdbf",)], key="wdc")
        scb = ar.alloc("sc" + sfx, [P, S], F32)
        Eb = ar.alloc("E" + sfx, [P, 4, S], BF16)
        ET = ar.alloc("ET" + sfx, [P, NT, 512], BF16)

        def emit_scores_softmax(t):
            g = t // 4
            wlen = (g + 1) * 512
            j = t % 4
            sl = t % 2
            for nb in range(g + 1):
                ap, kk = pf(nb * 512, 512)
                for ec in range(ND):
                    b.op("pe", lambda e, ap=ap, ec=ec, t=t, nb=nb:
                         e.matmul(ap, qT[:, ec, t * P:(t + 1) * P], kT[:, ec, nb * 512:(nb + 1) * 512],
                                  start=(ec == 0), stop=(ec == ND - 1)),
                         reads=[("qT" + sfx, ec, t // 4), ("kT" + sfx, ec, nb)], writes=list(kk))
                if nb == g:
                    b.op("dve", lambda e, ap=ap, nb=nb, j=j:
                         e.scalar_tensor_tensor(scb[:, nb * 512:(nb + 1) * 512], ap, INV_SQRT_DK,
                                                maskt[:, j, :], ALU.mult, ALU.add),
                         reads=list(kk) + [("maskt", j)], writes=[("sc" + sfx, nb)])
                else:
                    b.op("act", lambda e, ap=ap, nb=nb:
                         e.activation(scb[:, nb * 512:(nb + 1) * 512], ap, AF.Copy, scale=INV_SQRT_DK),
                         reads=list(kk), writes=[("sc" + sfx, nb)])
            sckeys = [("sc" + sfx, nb) for nb in range(g + 1)]
            b.op("dve", lambda e, wlen=wlen, sl=sl:
                 e.reduce_max(nmax[:, sl:sl + 1], scb[:, 0:wlen], axis=AX.X, negate=True),
                 reads=sckeys, writes=[("nmax", sl)])
            b.op("act", lambda e, wlen=wlen, sl=sl, t=t, j=j:
                 e.activation(Eb[:, j, 0:wlen], scb[:, 0:wlen], AF.Exp, bias=nmax[:, sl:sl + 1], scale=1.0,
                              accum_out=rs_all[:, t:t + 1]),
                 reads=sckeys + [("nmax", sl)], writes=[("E" + sfx, j), ("rs", t)])

        def emit_T(t):
            g = t // 4
            ncs = 4 * (g + 1)
            j = t % 4
            for c0 in range(0, ncs, 8):
                cb = min(8, ncs - c0)
                half = (c0 // 8) % 2
                for ci in range(cb):
                    cs = c0 + ci
                    apb, kkb = pbank(half, ci * P, P)
                    b.op("pe", lambda e, apb=apb, cs=cs, j=j:
                         e.transpose(apb, Eb[:, j, cs * P:(cs + 1) * P], identb[:]),
                         reads=[("E" + sfx, j), ("identb",)], writes=list(kkb))
                apb, kkb = pbank(half, 0, cb * P)
                dst = ET[:, c0:c0 + cb, j * P:(j + 1) * P]
                b.op("dve", lambda e, dst=dst, apb=apb: e.tensor_copy(dst, apb),
                     reads=list(kkb), writes=[("ET" + sfx, c0 + ci, j) for ci in range(cb)])

        def emit_AV(g):
            ncs = 4 * (g + 1)
            for vc in range(ND):
                ap, kk = pf((4 + vc % 2) * 512, 512)
                for cs in range(ncs):
                    b.op("pe", lambda e, ap=ap, cs=cs, vc=vc, ncs=ncs:
                         e.matmul(ap, vtm[:, cs, vc * P:(vc + 1) * P], ET[:, cs, :],
                                  start=(cs == 0), stop=(cs == ncs - 1)),
                         reads=[("v" + sfx, cs)] + [("ET" + sfx, cs, j) for j in range(4)],
                         writes=list(kk))
                b.op("dve", lambda e, ap=ap, vc=vc, g=g: e.tensor_copy(aoT[:, vc, g * 512:(g + 1) * 512], ap),
                     reads=list(kk), writes=[("aoT" + sfx, vc, g)])

        for g in range(NG):
            for j in range(4):
                emit_scores_softmax(4 * g + j)
            for j in range(4):
                emit_T(4 * g + j)
            emit_AV(g)
        b.op("dve", lambda e: e.reciprocal(ri_all[:], rs_all[:]),
             reads=[("rs", t) for t in range(NT)], writes=[("ri",)])

        ar.free("qT" + sfx)
        ar.free("kT" + sfx)
        ar.free("v" + sfx)
        ar.free("sc" + sfx)
        ar.free("E" + sfx)
        ar.free("ET" + sfx)

        r1 = ar.alloc("r1" + sfx, [P, 2, D], F32)
        ybuf = ar.alloc("y" + sfx, [P, 2, D], F32)
        ybb = ar.alloc("ybb" + sfx, [P, 2, D], BF16)
        yT = ar.alloc("yT" + sfx, [P, ND, S], BF16)

        for t in range(NT):
            sl = t % 2
            bz = 0 if t % 2 == 0 else 2
            b.dma("sp", lambda e, t=t, sl=sl: e.dma_start(out=xs[:, sl, :], in_=x_d.ap()[s, t * P:(t + 1) * P, :]),
                  writes=[("xs", sl)], key=f"xs{sl}")
            for half in range(2):
                for vc in range(ND):
                    ap, kk = pf((bz + half) * 512, 512)
                    b.op("pe", lambda e, ap=ap, vc=vc, t=t, half=half:
                         e.matmul(ap, aoT[:, vc, t * P:(t + 1) * P], wob[:, vc, half * 512:(half + 1) * 512],
                                  start=(vc == 0), stop=(vc == ND - 1)),
                         reads=[("wob" + sfx,), ("aoT" + sfx, vc, t // 4)], writes=list(kk))
            apz, kkz = pf(bz * 512, 1024)
            b.op("dve", lambda e, apz=apz, sl=sl, t=t:
                 e.scalar_tensor_tensor(r1[:, sl, :], apz, ri_all[:, t:t + 1], xs[:, sl, :], ALU.mult, ALU.add),
                 reads=list(kkz) + [("xs", sl), ("ri",)], writes=[("r1" + sfx, sl)])
            emit_ln(r1[:, sl, :], [("r1" + sfx, sl)], ybuf[:, sl, :], [("y" + sfx, sl)], sl, "dve")
            b.op("dve", lambda e, sl=sl: e.tensor_copy(ybb[:, sl, :], ybuf[:, sl, :]),
                 reads=[("y" + sfx, sl)], writes=[("ybb" + sfx, sl)])
            b.dma("sp", lambda e, t=t, sl=sl: e.dma_start(out=y_dram.ap()[t * P:(t + 1) * P, :], in_=ybuf[:, sl, :]),
                  reads=[("y" + sfx, sl)], writes=[("ydram", t)], key=f"yst{sl}")
            b.dma("sp", lambda e, t=t, sl=sl: e.dma_start(out=ybf_dram.ap()[t * P:(t + 1) * P, :], in_=ybb[:, sl, :]),
                  reads=[("ybb" + sfx, sl)], writes=[("ybf", t)], key=f"ybst{sl}")
            if t % 8 == 7:
                half_seq = t // 8
                for dc in range(ND):
                    b.dma("sp", lambda e, dc=dc, hs=half_seq:
                          e.dma_start_transpose(yT[:, dc, hs * 1024:(hs + 1) * 1024],
                                                ybf_dram.ap()[hs * 1024:(hs + 1) * 1024, dc * P:(dc + 1) * P]),
                          reads=[("ybf", tt) for tt in range(half_seq * 8, half_seq * 8 + 8)],
                          writes=[("yT" + sfx, dc, half_seq)], key=f"yTt{half_seq}", coarse=True)

        ar.free("aoT" + sfx)
        ar.free("wob" + sfx)
        ar.free("r1" + sfx)
        ar.free("y" + sfx)
        ar.free("ybb" + sfx)

        h_actT = ar.alloc("h" + sfx, [P, NH, S], BF16)
        wgTb = ar.alloc("wgTb" + sfx, [P, 2, ND, 512], BF16)
        wuTb = ar.alloc("wuTb" + sfx, [P, 2, ND, 512], BF16)
        sg = ar.alloc("sg" + sfx, [P, 2, 512], BF16)

        for hgrp in range(NH // 4):
            slw = hgrp % 2
            for dc in range(ND):
                b.dma("sp", lambda e, dc=dc, hgrp=hgrp, slw=slw:
                      e.dma_start_transpose(wgTb[:, slw, dc, :],
                                            wgbf_d.ap()[hgrp * 512:(hgrp + 1) * 512, dc * P:(dc + 1) * P]),
                      reads=[("wgbf",)], writes=[("wgTb" + sfx, slw)], key=f"wgt{slw}", coarse=True)
                b.dma("sp", lambda e, dc=dc, hgrp=hgrp, slw=slw:
                      e.dma_start_transpose(wuTb[:, slw, dc, :],
                                            wubf_d.ap()[hgrp * 512:(hgrp + 1) * 512, dc * P:(dc + 1) * P]),
                      reads=[("wubf",)], writes=[("wuTb" + sfx, slw)], key=f"wut{slw}", coarse=True)
            for hl in range(4):
                hc = hgrp * 4 + hl
                for npair in range(2):
                    for dc in range(ND):
                        for i in (0, 1):
                            n = 2 * npair + i
                            apg, kkg = pf((i) * 512, 512)
                            b.op("pe", lambda e, apg=apg, dc=dc, n=n, slw=slw, hl=hl:
                                 e.matmul(apg, wgTb[:, slw, dc, hl * P:(hl + 1) * P],
                                          yT[:, dc, n * 512:(n + 1) * 512],
                                          start=(dc == 0), stop=(dc == ND - 1)),
                                 reads=[("wgTb" + sfx, slw), ("yT" + sfx, dc, n // 2)],
                                 writes=list(kkg))
                    for dc in range(ND):
                        for i in (0, 1):
                            n = 2 * npair + i
                            apu, kku = pf((2 + i) * 512, 512)
                            b.op("pe", lambda e, apu=apu, dc=dc, n=n, slw=slw, hl=hl:
                                 e.matmul(apu, wuTb[:, slw, dc, hl * P:(hl + 1) * P],
                                          yT[:, dc, n * 512:(n + 1) * 512],
                                          start=(dc == 0), stop=(dc == ND - 1)),
                                 reads=[("wuTb" + sfx, slw), ("yT" + sfx, dc, n // 2)],
                                 writes=list(kku))
                    for i in (0, 1):
                        n = 2 * npair + i
                        apg, kkg = pf(i * 512, 512)
                        apu, kku = pf((2 + i) * 512, 512)
                        b.op("act", lambda e, apg=apg, n=n: e.activation(sg[:, n % 2, :], apg, AF.Silu, scale=DEQ),
                             reads=list(kkg), writes=[("sg" + sfx, n % 2)])
                        b.op("dve", lambda e, apu=apu, n=n, hc=hc:
                             e.tensor_tensor(h_actT[:, hc, n * 512:(n + 1) * 512], apu, sg[:, n % 2, :], ALU.mult),
                             reads=list(kku) + [("sg" + sfx, n % 2)], writes=[("h" + sfx, hc, n)])

        ar.free("yT" + sfx)
        ar.free("wgTb" + sfx)
        ar.free("wuTb" + sfx)
        ar.free("sg" + sfx)

        wdT = ar.alloc("wdT" + sfx, [P, NH, D], BF16)
        ys = ar.alloc("ys" + sfx, [P, 2, D], F32)
        r2 = ar.alloc("r2" + sfx, [P, 2, D], F32)
        outb = ar.alloc("outb" + sfx, [P, 2, D], F32)

        for hc in range(NH):
            b.dma("sp", lambda e, hc=hc: e.dma_start_transpose(wdT[:, hc, :], wdbf_d.ap()[:, hc * P:(hc + 1) * P]),
                  reads=[("wdbf",)], writes=[("wdT" + sfx, hc)], key="wdt", coarse=True)

        for t in range(NT):
            sl = t % 2
            bz = 0 if t % 2 == 0 else 2
            b.dma("sp", lambda e, t=t, sl=sl: e.dma_start(out=ys[:, sl, :], in_=y_dram.ap()[t * P:(t + 1) * P, :]),
                  reads=[("ydram", t)], writes=[("ys" + sfx, sl)], key=f"ys{sl}")
            for hc in range(NH):
                for half in range(2):
                    ap, kk = pf((bz + half) * 512, 512)
                    b.op("pe", lambda e, ap=ap, hc=hc, t=t, half=half:
                         e.matmul(ap, h_actT[:, hc, t * P:(t + 1) * P], wdT[:, hc, half * 512:(half + 1) * 512],
                                  start=(hc == 0), stop=(hc == NH - 1)),
                         reads=[("h" + sfx, hc, t // 4), ("wdT" + sfx, hc)], writes=list(kk))
            apz, kkz = pf(bz * 512, 1024)
            b.op("dve", lambda e, apz=apz, sl=sl:
                 e.scalar_tensor_tensor(r2[:, sl, :], apz, DEQ * DEQ, ys[:, sl, :], ALU.mult, ALU.add),
                 reads=list(kkz) + [("ys" + sfx, sl)], writes=[("r2" + sfx, sl)])
            emit_ln(r2[:, sl, :], [("r2" + sfx, sl)], outb[:, sl, :], [("outb" + sfx, sl)], sl, "dve")
            b.dma("sp", lambda e, t=t, sl=sl: e.dma_start(out=out_d.ap()[s, t * P:(t + 1) * P, :], in_=outb[:, sl, :]),
                  reads=[("outb" + sfx, sl)], key=f"ost{sl}")

        ar.free("wdT" + sfx)
        ar.free("h" + sfx)
        ar.free("ys" + sfx)
        ar.free("r2" + sfx)
        ar.free("outb" + sfx)

    for s in range(SEQ_PER_CORE):
        emit_seq(s)

    stats = b.finalize_and_emit(final_wait_keys=["ost0", "ost1"])
    stats["sbuf_peak"] = ar.peak
    return nc, stats


def kernel(L0_attn_x, L0_attn_Wq, L0_attn_Wk, L0_attn_Wv, L0_attn_Wo,
           L0_ffn_w_gate, L0_ffn_w_up, L0_ffn_w_down):
    x = np.ascontiguousarray(np.asarray(L0_attn_x, dtype=np.float32))
    wq = np.ascontiguousarray(np.asarray(L0_attn_Wq, dtype=np.float32))
    wk = np.ascontiguousarray(np.asarray(L0_attn_Wk, dtype=np.float32))
    wv = np.ascontiguousarray(np.asarray(L0_attn_Wv, dtype=np.float32))
    wo = np.ascontiguousarray(np.asarray(L0_attn_Wo, dtype=np.float32))
    wg = np.ascontiguousarray(np.asarray(L0_ffn_w_gate, dtype=np.int32))
    wu = np.ascontiguousarray(np.asarray(L0_ffn_w_up, dtype=np.int32))
    wd = np.ascontiguousarray(np.asarray(L0_ffn_w_down, dtype=np.int32))

    nc, stats = build_program()
    in_maps = []
    for c in range(N_CORES):
        in_maps.append({
            "x": x[c * SEQ_PER_CORE:(c + 1) * SEQ_PER_CORE],
            "wq": wq, "wk": wk, "wv": wv, "wo": wo,
            "wg": wg, "wu": wu, "wd": wd,
        })
    res = run_bass_kernel_spmd(nc, in_maps, core_ids=list(range(N_CORES)))
    out = np.concatenate([r["out"] for r in res.results], axis=0)
    return out.astype(np.float32)


if __name__ == "__main__":
    nc, stats = build_program()
    print("build ok:", stats)
